# version 17
# speedup vs baseline: 1.1652x; 1.0218x over previous
import math
from contextlib import ExitStack

import numpy as np
import ml_dtypes

import concourse.bass as bass
import concourse.mybir as mybir
from concourse.bass_utils import run_bass_kernel_spmd

F32 = mybir.dt.float32
BF16 = mybir.dt.bfloat16
I32 = mybir.dt.int32
AF = mybir.ActivationFunctionType
ALU = mybir.AluOpType
AX = mybir.AxisListType
BF = ml_dtypes.bfloat16

NCORES = 8
B_, S_, D_ = 2, 8192, 1024
TC = 2048
H_, KD, VD, HD = 8, 128, 2048, 256
CH = 128
NCHUNK = TC // CH
EPS = 1e-5
E_, N_, DTR, KC_ = 2048, 16, 64, 4
ESL = E_ // 4
HID0 = 3072
HID1 = 4096

ENGINES = ("sync", "scalar", "vector", "tensor", "gpsimd")


class Op:
    __slots__ = ("engine", "fn", "deps", "dma", "gid", "seq", "needs_inc",
                 "milestone", "tag", "waits", "hard", "slot")

    def __init__(self, engine, fn, deps, dma, gid, tag):
        self.engine = engine
        self.fn = fn
        self.deps = list(deps)
        self.dma = dma
        self.gid = gid
        self.seq = -1
        self.needs_inc = False
        self.milestone = -1
        self.tag = tag
        self.hard = set()
        self.slot = None


class Prog:

    def __init__(self, nc):
        self.nc = nc
        self.ops = []
        self.last_eng = {}
        self.last_slot = {}
        self.rank_engines = set()
        self.regs = {}
        self.stage = ""
        self.ns = ""

    def op(self, engine, fn, deps=(), tag=None, hard_deps=()):
        deps = [d for d in deps if d is not None]
        hard_deps = [d for d in hard_deps if d is not None]
        o = Op(engine, fn, deps + hard_deps, False, len(self.ops), tag)
        o.hard = {d.gid for d in hard_deps}
        for d in o.deps:
            assert d.gid < o.gid, f"dep added after op ({tag})"
        self.ops.append(o)
        self.last_eng[engine] = o
        return o

    def parallel(self, emit_fns, chunk=14, tag="par"):
        base = len(self.ops)
        lanes = []
        for fn in emit_fns:
            start = len(self.ops)
            fn()
            lanes.append(self.ops[start:])
        for lane in lanes:
            lanes_set = {o.gid for o in lane}
            for o in lane:
                for dd in o.deps:
                    assert dd.gid < base or dd.gid in lanes_set,                         f"cross-lane dep {o.tag} <- {dd.tag}"
        chunks = [chunk] * len(lanes) if isinstance(chunk, int) else list(chunk)
        tail = []
        idx = [0] * len(lanes)
        while any(idx[i] < len(lanes[i]) for i in range(len(lanes))):
            for i, lane in enumerate(lanes):
                tail.extend(lane[idx[i]:idx[i] + chunks[i]])
                idx[i] += chunks[i]
        self.ops[base:] = tail
        for g, o in enumerate(self.ops):
            o.gid = g

    def set_stage(self, stage, ns="L0."):
        self.stage = stage
        self.ns = ns

    def barrier_all(self, tag="stage"):
        deps = list(self.last_eng.values()) + list(self.last_slot.values())
        outs = []
        for e in ENGINES:
            outs.append(self.op(e, lambda eng: eng.nop(), deps=deps, tag=f"bar_{tag}_{e}"))
        return outs

    def dma(self, engine, fn, deps=(), tag=None, slot=None):
        assert slot is not None, f"dma needs slot ({tag})"
        if slot.startswith("f_"):
            slot = self.stage + slot
        else:
            slot = self.ns + slot
        o = self.op(engine, fn, deps, tag)
        o.dma = True
        o.slot = slot
        self.last_slot[slot] = o
        return o

    def emit(self, es, block):
        nc = self.nc
        sems = {e: es.enter_context(nc.semaphore(f"sem_{e}")) for e in ENGINES}
        slots = []
        for o in self.ops:
            if o.dma and o.slot not in slots:
                slots.append(o.slot)
        ssems = {s: es.enter_context(nc.semaphore(f"sl_{i}")) for i, s in enumerate(slots)}
        per_eng = {e: [] for e in ENGINES}
        for o in self.ops:
            o.seq = len(per_eng[o.engine])
            per_eng[o.engine].append(o)
        for o in self.ops:
            for d in o.deps:
                d.needs_inc = True
        slot_cnt = {s: 0 for s in slots}
        for o in self.ops:
            if o.dma:
                slot_cnt[o.slot] += 16
                o.milestone = slot_cnt[o.slot]
        for e, lst in per_eng.items():
            cnt = 0
            for o in lst:
                if o.needs_inc and not o.dma:
                    cnt += 1
                    o.milestone = cnt
        for o in self.ops:
            waits = {}
            for d in o.deps:
                if d.dma:
                    key = ("slot", d.slot)
                    val = slot_cnt[d.slot] if d.slot.split(".")[-1].startswith("f_") else d.milestone
                else:
                    key = ("eng", d.engine)
                    val = d.milestone
                waits[key] = max(waits.get(key, 0), val)
            o.waits = waits

        def make_cb(e):
            def cb(eng):
                if e in self.rank_engines:
                    rk = eng.alloc_register(f"rk_{e}")
                    tmp = eng.alloc_register(f"tmp_{e}")
                    self.regs[e] = (rk, tmp)
                for o in per_eng[e]:
                    for (kind, key), val in o.waits.items():
                        eng.wait_ge(ssems[key] if kind == "slot" else sems[key], val)
                    inst = o.fn(eng)
                    if o.dma:
                        inst.then_inc(ssems[o.slot], 16)
                    elif o.needs_inc:
                        assert inst is not None, f"op {o.tag} needs_inc but no inst"
                        inst.then_inc(sems[o.engine], 1)
            return cb

        for e in ENGINES:
            if per_eng[e]:
                getattr(block, e)(make_cb(e))

    def barrier_op(self, engine, deps, tag="barrier"):
        def fn(eng):
            return eng.nop()
        return self.op(engine, fn, deps, tag=tag)


_CV_N = [0]


class CVals:

    def __init__(self, nc, es, p, values, engine="vector", name=None):
        if name is None:
            _CV_N[0] += 1
            name = f"cvals{_CV_N[0]}"
        values = [float(v) for v in values]
        self.tile = es.enter_context(nc.sbuf_tensor(name, [128, max(1, len(values))], F32))
        self.map = {}
        for i, v in enumerate(values):
            op = p.op(engine,
                      lambda e, i=i, v=v: e.memset(self.tile[:, i:i + 1], v),
                      tag=f"{name}{i}")
            self.map[v] = (self.tile[:, i:i + 1], op)

    def ap(self, v):
        return self.map[float(v)][0]

    def dep(self, v):
        return self.map[float(v)][1]


_CONSTS = None


def _consts():
    global _CONSTS
    if _CONSTS is not None:
        return _CONSTS
    lin = np.linspace(0.0, 1.0, KD // 2)
    angle = np.repeat(1.0 / 10000.0 ** lin, 2)
    idx = np.arange(S_, dtype=np.float64)
    sin = np.sin(idx[:, None] * angle[None, :])
    cos = np.cos(idx[:, None] * angle[None, :])
    gamma = 1.0 - 2.0 ** (-5.0 - np.arange(H_, dtype=np.float64))
    Rtab = np.sqrt((1.0 - gamma[:, None] ** (idx[None, :] + 1.0))
                   / (1.0 - gamma[:, None]))
    a = np.arange(CH, dtype=np.float64)
    DmT = np.where(a[None, :, None] <= a[None, None, :],
                   gamma[:, None, None] ** (a[None, None, :] - a[None, :, None]),
                   0.0)
    gq = gamma[:, None] ** (a[None, :] + 1.0)
    gk = gamma[:, None] ** (CH - 1.0 - a[None, :])
    gq_t = np.ascontiguousarray(np.broadcast_to(gq[:, None, :], (H_, 128, CH))).astype(BF)
    gk_t = np.ascontiguousarray(np.broadcast_to(gk[:, None, :], (H_, 128, CH))).astype(BF)
    RT = np.zeros((128, 128), np.float32)
    for t in range(64):
        RT[2 * t + 1, 2 * t] = -1.0
        RT[2 * t, 2 * t + 1] = 1.0
    gktm = np.ascontiguousarray(gk.T).astype(np.float32)
    _CONSTS = dict(
        sin=sin, cos=cos, gamma=gamma, Rtab=Rtab,
        DmT=np.ascontiguousarray(DmT).astype(BF), gq_t=gq_t, gk_t=gk_t,
        RT=RT.astype(BF), ident=np.eye(128, dtype=BF), gktm=gktm,
    )
    return _CONSTS


def _tok0(c):
    return (c % 4) * TC


def _io_A(nc):
    io = {}
    io["x"] = nc.dram_tensor("x", [TC, D_], F32, kind="ExternalInput").ap()
    for nm in ("cosq", "sinq", "cosk", "sink"):
        io[nm] = nc.dram_tensor(nm, [KD, TC], BF16, kind="ExternalInput").ap()
    io["costm"] = nc.dram_tensor("costm", [128, (TC // 128) * D_], BF16, kind="ExternalInput").ap()
    io["sintm"] = nc.dram_tensor("sintm", [128, (TC // 128) * D_], BF16, kind="ExternalInput").ap()
    io["krtm"] = nc.dram_tensor("krtm", [TC, D_], BF16, kind="ExternalOutput").ap()
    io["wq"] = nc.dram_tensor("wq", [D_, D_], F32, kind="ExternalInput").ap()
    io["wk"] = nc.dram_tensor("wk", [D_, D_], F32, kind="ExternalInput").ap()
    io["wv"] = nc.dram_tensor("wv", [D_, VD], F32, kind="ExternalInput").ap()
    io["wg"] = nc.dram_tensor("wg", [D_, VD], F32, kind="ExternalInput").ap()
    io["rt"] = nc.dram_tensor("rt", [128, 128], BF16, kind="ExternalInput").ap()
    io["ident"] = nc.dram_tensor("ident", [128, 128], BF16, kind="ExternalInput").ap()
    io["xn_fm"] = nc.dram_tensor("xn_fm", [D_, TC], BF16, kind="ExternalOutput").ap()
    io["qr"] = nc.dram_tensor("qr", [D_, TC], BF16, kind="ExternalOutput").ap()
    io["kr"] = nc.dram_tensor("kr", [D_, TC], BF16, kind="ExternalOutput").ap()
    io["v"] = nc.dram_tensor("v", [TC, VD], BF16, kind="ExternalOutput").ap()
    io["g"] = nc.dram_tensor("g", [TC, VD], BF16, kind="ExternalOutput").ap()
    return io


def emit_A(nc, p, io):
    x_d = io["x"]
    wq_d = io["wq"]
    wk_d = io["wk"]
    wv_d = io["wv"]
    wg_d = io["wg"]
    cs_d = {nm: io[nm] for nm in ("cosq", "sinq", "cosk", "sink")}
    costm_d = io["costm"]
    sintm_d = io["sintm"]
    krtm_d = io["krtm"]
    rt_d = io["rt"]
    id_d = io["ident"]
    xnf_d = io["xn_fm"]
    qr_d = io["qr"]
    kr_d = io["kr"]
    v_d = io["v"]
    g_d = io["g"]
    NTC = TC // 128
    es_own = "es" not in io
    es = io.get("es") or ExitStack()
    try:
        xt = [es.enter_context(nc.sbuf_tensor("A." + f"xt{i}", [128, D_], F32)) for i in range(2)]
        xsq = es.enter_context(nc.sbuf_tensor("A." + "xsq", [128, D_], BF16))
        red = es.enter_context(nc.sbuf_tensor("A." + "red", [128, 16], F32))
        xn_tm = [es.enter_context(nc.sbuf_tensor("A." + f"xntm{i}", [128, D_], BF16)) for i in range(2)]
        xn_fm = [es.enter_context(nc.sbuf_tensor("A." + f"xnf{i}", [128, TC], BF16)) for i in range(8)]
        wbuf = es.enter_context(nc.sbuf_tensor("A." + "wbuf", [128, 8 * VD], BF16))
        wst = [es.enter_context(nc.sbuf_tensor("A." + f"wst{i}", [128, VD], F32)) for i in range(2)]
        cs = {nm: es.enter_context(nc.sbuf_tensor("A." + "s_" + nm, [128, TC], BF16))
              for nm in ("cosq", "sinq", "cosk", "sink")}
        costm = es.enter_context(nc.sbuf_tensor("A." + "costm", [128, (TC // 128) * D_], BF16))
        sintm = es.enter_context(nc.sbuf_tensor("A." + "sintm", [128, (TC // 128) * D_], BF16))
        ktmt = [es.enter_context(nc.sbuf_tensor("A." + f"ktmt{i}", [128, 1024], BF16)) for i in range(2)]
        krot = es.enter_context(nc.sbuf_tensor("A." + "krot", [128, 1024], BF16))
        krot2 = es.enter_context(nc.sbuf_tensor("A." + "krot2", [128, 1024], BF16))
        kct = es.enter_context(nc.sbuf_tensor("A." + "kct", [128, 1024], F32))
        kres = [es.enter_context(nc.sbuf_tensor("A." + f"kres{i}", [128, 1024], BF16)) for i in range(2)]
        rt = es.enter_context(nc.sbuf_tensor("A." + "s_rt", [128, 128], BF16))
        ident = es.enter_context(nc.sbuf_tensor("A." + "s_ident", [128, 128], BF16))
        qsb = [es.enter_context(nc.sbuf_tensor("A." + f"qsb{i}", [128, 512], BF16)) for i in range(2)]
        tt1 = es.enter_context(nc.sbuf_tensor("A." + "tt1", [128, 512], F32))
        tt2 = es.enter_context(nc.sbuf_tensor("A." + "tt2", [128, 512], F32))
        qres = [es.enter_context(nc.sbuf_tensor("A." + f"qres{i}", [128, 512], BF16)) for i in range(2)]
        vres = [es.enter_context(nc.sbuf_tensor("A." + f"vres{i}", [128, 512], BF16)) for i in range(2)]
        ps_mm = [es.enter_context(nc.psum_tensor("A." + f"psmm{i}", [128, 512], F32)) for i in range(2)]
        ps_tr = [es.enter_context(nc.psum_tensor("A." + f"pstr{i}", [128, 128], BF16)) for i in range(2)]
        ps_rq = [es.enter_context(nc.psum_tensor("A." + f"psrq{i}", [128, 512], F32)) for i in range(2)]
        CV = CVals(nc, es, p, [EPS])

        c_loads = []
        for nm in cs:
            c_loads.append(p.dma("sync", lambda e, a=cs[nm], b=cs_d[nm]: e.dma_start(out=a[:], in_=b), tag=f"ld_{nm}", slot="f_init"))
        ld_rt = p.dma("sync", lambda e: e.dma_start(out=rt[:], in_=rt_d), tag="ld_rt", slot="f_init")
        c_loads.append(p.dma("sync", lambda e: e.dma_start(out=costm[:], in_=costm_d), tag="ld_costm", slot="f_init"))
        c_loads.append(p.dma("sync", lambda e: e.dma_start(out=sintm[:], in_=sintm_d), tag="ld_sintm", slot="f_init"))
        ld_id = p.dma("sync", lambda e: e.dma_start(out=ident[:], in_=id_d), tag="ld_id", slot="f_init")

        tr_done = []
        prev_sq = None
        prev_region = [(None, None), (None, None)]
        prev_tr_evac = [None, None]
        prev_xn_use = [None, None]
        for i in range(NTC):
            rb = (i % 2) * 8

            def rc(j, rb=rb):
                return red[:, rb + j:rb + j + 1]

            pxn, psqr = prev_region[i % 2]
            ldx = p.dma("sync",
                        lambda e, i=i: e.dma_start(out=xt[i % 2][:], in_=x_d[i * 128:(i + 1) * 128, :]),
                        deps=[prev_xn_use[i % 2]], tag=f"ldx{i}", slot=f"xt{i % 2}")
            sq = p.op("scalar",
                      lambda e, i=i, rc=rc: e.activation(out=xsq[:], in_=xt[i % 2][:], func=AF.Square,
                                                         accum_out=rc(1)),
                      deps=[ldx, prev_sq, pxn], tag=f"sq{i}")
            rs = p.op("vector",
                      lambda e, i=i, rc=rc: e.tensor_reduce(out=rc(0), in_=xt[i % 2][:],
                                                            axis=AX.X, op=ALU.add),
                      deps=[ldx, pxn, psqr], tag=f"rs{i}")
            m_ = p.op("vector",
                      lambda e, rc=rc: e.tensor_scalar(out=rc(2), in0=rc(0),
                                                       scalar1=1.0 / D_, scalar2=None, op0=ALU.mult),
                      deps=[rs], tag=f"m{i}")
            m2 = p.op("vector",
                      lambda e, rc=rc: e.tensor_tensor(out=rc(5), in0=rc(2),
                                                       in1=rc(2), op=ALU.mult),
                      deps=[m_], tag=f"m2_{i}")
            var = p.op("vector",
                       lambda e, rc=rc: e.scalar_tensor_tensor(out=rc(3), in0=rc(1),
                                                               scalar=1.0 / D_, in1=rc(5),
                                                               op0=ALU.mult, op1=ALU.subtract),
                       deps=[m2, sq], tag=f"var{i}")
            sqr = p.op("scalar",
                       lambda e, rc=rc: e.activation(out=rc(4), in_=rc(3), func=AF.Sqrt,
                                                     bias=CV.ap(EPS)),
                       deps=[var, CV.dep(EPS)], tag=f"sqrt{i}")
            rcp = p.op("vector",
                       lambda e, rc=rc: e.reciprocal(out=rc(6), in_=rc(4)),
                       deps=[sqr], tag=f"rcp{i}")
            msr = p.op("vector",
                       lambda e, rc=rc: e.tensor_tensor(out=rc(7), in0=rc(2),
                                                        in1=rc(6), op=ALU.mult),
                       deps=[rcp], tag=f"msr{i}")
            xn = p.op("vector",
                      lambda e, i=i, rc=rc: e.tensor_scalar(out=xn_tm[i % 2][:], in0=xt[i % 2][:],
                                                            scalar1=rc(6), scalar2=rc(7),
                                                            op0=ALU.mult, op1=ALU.subtract),
                      deps=[msr, prev_tr_evac[i % 2]], tag=f"xn{i}")
            prev_region[i % 2] = (xn, sqr)
            prev_xn_use[i % 2] = xn
            prev_sq = sq
            evs = []
            for dc in range(8):
                tr = p.op("tensor",
                          lambda e, i=i, dc=dc: e.transpose(out=ps_tr[dc % 2][:],
                                                            in_=xn_tm[i % 2][:, dc * 128:(dc + 1) * 128],
                                                            identity=ident[:]),
                          deps=[xn, ld_id] + ([evs[dc - 2]] if dc >= 2 else ([tr_done[-1]] if tr_done else [])),
                          tag=f"tr{i}_{dc}")
                ev_ = p.op("vector",
                           lambda e, i=i, dc=dc: e.tensor_copy(out=xn_fm[dc][:, i * 128:(i + 1) * 128],
                                                               in_=ps_tr[dc % 2][:]),
                           deps=[tr], tag=f"trev{i}_{dc}")
                evs.append(ev_)
            prev_tr_evac[i % 2] = evs[-1]
            tr_done.append(evs[-1])
        ln_all = p.barrier_op("tensor", tr_done, tag="ln_done")

        st_xn = []
        for dc in range(8):
            st_xn.append(p.dma("gpsimd",
                               lambda e, dc=dc: e.dma_start(out=xnf_d[dc * 128:(dc + 1) * 128, :], in_=xn_fm[dc][:]),
                               deps=[tr_done[-1], ln_all], tag=f"stxn{dc}", slot="f_st"))

        projs = [("q", wq_d, D_, qr_d), ("k", wk_d, D_, kr_d),
                 ("v", wv_d, VD, v_d), ("g", wg_d, VD, g_d),
                 ("ktm", wk_d, D_, krtm_d)]
        prev_conv = [None, None]
        stores = list(st_xn)
        prev_mm_user = {}
        prev_rqm = [None, None]
        prev_q_user = [None, None]
        prev_qres_user = [None, None]
        prev_vres_user = [None, None]
        last_mm_prev_proj = None
        jj = 0
        for pi, (pname, wd, width, out_d) in enumerate(projs):
            convs = []
            for kc in range(8):
                j = jj
                jj += 1
                ldw = p.dma("sync",
                            lambda e, wd=wd, kc=kc, j=j, width=width:
                            e.dma_start(out=wst[j % 2][:, 0:width], in_=wd[kc * 128:(kc + 1) * 128, :]),
                            deps=[prev_conv[j % 2]], tag=f"ldw{pname}{kc}", slot=f"wst{j % 2}")
                cv = p.op("scalar",
                          lambda e, kc=kc, j=j, width=width:
                          e.activation(out=wbuf[:, kc * VD:kc * VD + width],
                                       in_=wst[j % 2][:, 0:width], func=AF.Copy),
                          deps=[ldw, last_mm_prev_proj] + ([convs[-1]] if convs else []),
                          tag=f"cv{pname}{kc}")
                prev_conv[j % 2] = cv
                convs.append(cv)
            wready = convs[-1]
            if pname in ("q", "k"):
                cosn = cs["cosq"] if pname == "q" else cs["cosk"]
                sinn = cs["sinq"] if pname == "q" else cs["sink"]
                prev_c3 = None
                for h in range(H_):
                    for tb in range(4):
                        s = tb * 512
                        idx2 = h * 4 + tb
                        slot = idx2 % 2
                        mms = []
                        for kc in range(8):
                            mmop = p.op("tensor",
                                        lambda e, h=h, kc=kc, s=s, slot=slot:
                                        e.matmul(out=ps_mm[slot][:],
                                                 lhsT=wbuf[:, kc * VD + h * 128:kc * VD + (h + 1) * 128],
                                                 rhs=xn_fm[kc][:, s:s + 512],
                                                 start=(kc == 0), stop=(kc == 7)),
                                        deps=([convs[kc], tr_done[4 * tb + 3], prev_mm_user.get(slot)]
                                              if kc == 0 else [convs[kc]]),
                                        tag=f"{pname}mm{h}_{tb}_{kc}")
                            mms.append(mmop)
                        evq = p.op("vector",
                                   lambda e, slot=slot: e.tensor_copy(out=qsb[slot][:], in_=ps_mm[slot][:]),
                                   deps=[mms[-1], prev_rqm[slot]], tag=f"{pname}ev{h}_{tb}")
                        prev_mm_user[slot] = evq
                        rqm = p.op("tensor",
                                   lambda e, slot=slot: e.matmul(out=ps_rq[slot][:], lhsT=rt[:],
                                                                 rhs=qsb[slot][:], start=True, stop=True),
                                   deps=[evq, ld_rt, prev_q_user[slot]], tag=f"{pname}rq{h}_{tb}")
                        prev_rqm[slot] = rqm
                        c1 = p.op("vector",
                                  lambda e, slot=slot, s=s, cosn=cosn:
                                  e.tensor_tensor(out=tt1[:], in0=qsb[slot][:],
                                                  in1=cosn[:, s:s + 512], op=ALU.mult),
                                  deps=[evq, prev_c3] + c_loads, tag=f"{pname}c1{h}_{tb}")
                        c2 = p.op("vector",
                                  lambda e, slot=slot, s=s, sinn=sinn:
                                  e.tensor_tensor(out=tt2[:], in0=ps_rq[slot][:],
                                                  in1=sinn[:, s:s + 512], op=ALU.mult),
                                  deps=[rqm, prev_c3], tag=f"{pname}c2{h}_{tb}")
                        prev_q_user[slot] = c2
                        c3 = p.op("vector",
                                  lambda e, slot=slot: e.tensor_tensor(out=qres[slot][:], in0=tt1[:],
                                                                       in1=tt2[:], op=ALU.add),
                                  deps=[c1, c2, prev_qres_user[slot]], tag=f"{pname}c3{h}_{tb}")
                        prev_c3 = c3
                        sto = p.dma("gpsimd",
                                    lambda e, out_d=out_d, h=h, s=s, slot=slot:
                                    e.dma_start(out=out_d[h * 128:(h + 1) * 128, s:s + 512], in_=qres[slot][:]),
                                    deps=[c3], tag=f"{pname}st{h}_{tb}", slot=f"qres{slot}")
                        prev_qres_user[slot] = sto
                        stores.append(sto)
                        last_mm_prev_proj = rqm
            elif pname == "ktm":
                prev_ktm = [None, None]
                prev_kres = [None, None]
                prev_krot = None
                prev_kct = None
                for tc in range(NTC):
                    kpar = tc % 2
                    mms = []
                    for ob in range(2):
                        slot = ob
                        for kc in range(8):
                            mms.append(p.op("tensor",
                                            lambda e, tc=tc, ob=ob, kc=kc, slot=slot:
                                            e.matmul(out=ps_mm[slot][:],
                                                     lhsT=xn_fm[kc][:, tc * 128:(tc + 1) * 128],
                                                     rhs=wbuf[:, kc * VD + ob * 512:kc * VD + ob * 512 + 512],
                                                     start=(kc == 0), stop=(kc == 7)),
                                            deps=([convs[kc], tr_done[tc], prev_mm_user.get(slot)]
                                                  if kc == 0 else [convs[kc]]),
                                            tag=f"ktmm{tc}_{ob}_{kc}"))
                        evk = p.op("vector",
                                   lambda e, ob=ob, slot=slot, kpar=kpar:
                                   e.tensor_copy(out=ktmt[kpar][:, ob * 512:ob * 512 + 512],
                                                 in_=ps_mm[slot][:]),
                                   deps=[mms[-1], prev_ktm[kpar] if ob == 0 else None],
                                   tag=f"ktev{tc}_{ob}")
                        prev_mm_user[slot] = evk
                    r1 = p.op("vector",
                              lambda e, kpar=kpar, tc=tc:
                              e.tensor_scalar(out=krot[:, 0:1024:2], in0=ktmt[kpar][:, 1:1024:2],
                                              scalar1=-1.0, scalar2=None, op0=ALU.mult),
                              deps=[evk, prev_krot], tag=f"ktr1{tc}")
                    r2 = p.op("vector",
                              lambda e, kpar=kpar: e.tensor_copy(out=krot[:, 1:1024:2],
                                                                 in_=ktmt[kpar][:, 0:1024:2]),
                              deps=[evk], tag=f"ktr2{tc}")
                    c1k = p.op("vector",
                               lambda e, kpar=kpar, tc=tc:
                               e.tensor_tensor(out=kct[:], in0=ktmt[kpar][:],
                                               in1=costm[:, tc * 1024:(tc + 1) * 1024],
                                               op=ALU.mult),
                               deps=[evk, prev_kct] + c_loads, tag=f"ktc1{tc}")
                    prev_ktm[kpar] = c1k
                    c2k = p.op("vector",
                               lambda e, tc=tc: e.tensor_tensor(out=krot2[:], in0=krot[:],
                                                                in1=sintm[:, tc * 1024:(tc + 1) * 1024],
                                                                op=ALU.mult),
                               deps=[r1, r2, c1k], tag=f"ktc2{tc}")
                    prev_krot = c2k
                    c3k = p.op("vector",
                               lambda e, kpar=kpar: e.tensor_tensor(out=kres[kpar][:], in0=kct[:],
                                                                    in1=krot2[:], op=ALU.add),
                               deps=[c2k, prev_kres[kpar]], tag=f"ktc3{tc}")
                    sto = p.dma("gpsimd",
                                lambda e, tc=tc, kpar=kpar:
                                e.dma_start(out=out_d[tc * 128:(tc + 1) * 128, :], in_=kres[kpar][:]),
                                deps=[c3k], tag=f"ktst{tc}", slot=f"kres{kpar}")
                    prev_kres[kpar] = sto
                    prev_kct = c3k
                    stores.append(sto)
                    last_mm_prev_proj = mms[-1]
            else:
                for tc in range(NTC):
                    for ob in range(width // 512):
                        idx2 = tc * (width // 512) + ob
                        slot = idx2 % 2
                        mms = []
                        for kc in range(8):
                            mmop = p.op("tensor",
                                        lambda e, tc=tc, ob=ob, kc=kc, slot=slot:
                                        e.matmul(out=ps_mm[slot][:],
                                                 lhsT=xn_fm[kc][:, tc * 128:(tc + 1) * 128],
                                                 rhs=wbuf[:, kc * VD + ob * 512:kc * VD + ob * 512 + 512],
                                                 start=(kc == 0), stop=(kc == 7)),
                                        deps=([convs[kc], tr_done[tc], prev_mm_user.get(slot)]
                                              if kc == 0 else [convs[kc]]),
                                        tag=f"{pname}mm{tc}_{ob}_{kc}")
                            mms.append(mmop)
                        evv = p.op("vector",
                                   lambda e, slot=slot: e.tensor_copy(out=vres[slot][:], in_=ps_mm[slot][:]),
                                   deps=[mms[-1], prev_vres_user[slot]], tag=f"{pname}ev{tc}_{ob}")
                        prev_mm_user[slot] = evv
                        sto = p.dma("gpsimd",
                                    lambda e, out_d=out_d, tc=tc, ob=ob, slot=slot:
                                    e.dma_start(out=out_d[tc * 128:(tc + 1) * 128, ob * 512:ob * 512 + 512],
                                                in_=vres[slot][:]),
                                    deps=[evv], tag=f"{pname}st{tc}_{ob}", slot=f"vres{slot}")
                        prev_vres_user[slot] = sto
                        stores.append(sto)
                        last_mm_prev_proj = mms[-1]

        p.barrier_op("gpsimd", stores, tag="final")
    finally:
        if es_own:
            es.close()


def build_A():
    nc = bass.Bass("TRN2", target_bir_lowering=False, debug=False)
    io = _io_A(nc)
    p = Prog(nc)
    p.set_stage("A.")
    emit_A(nc, p, io)
    with ExitStack() as es:
        block = es.enter_context(nc.Block())
        p.emit(es, block)
    return nc


def _inputs_A(inputs, c):
    C = _consts()
    b, t0 = c // 4, _tok0(c)
    x = np.ascontiguousarray(inputs["L0_attn_x"][b, t0:t0 + TC]).astype(np.float32)
    sc = KD ** -0.5
    cos = C["cos"][t0:t0 + TC].T
    sin = C["sin"][t0:t0 + TC].T
    ctm = C["cos"][t0:t0 + TC] * sc
    stm = C["sin"][t0:t0 + TC] * sc
    ctm8 = np.ascontiguousarray(
        np.broadcast_to(ctm.reshape(TC // 128, 128, 1, 128), (TC // 128, 128, 8, 128))
        .transpose(1, 0, 2, 3)).reshape(128, (TC // 128) * D_)
    stm8 = np.ascontiguousarray(
        np.broadcast_to(stm.reshape(TC // 128, 128, 1, 128), (TC // 128, 128, 8, 128))
        .transpose(1, 0, 2, 3)).reshape(128, (TC // 128) * D_)
    return {
        "x": x,
        "wq": inputs["L0_attn_wq"], "wk": inputs["L0_attn_wk"],
        "wv": inputs["L0_attn_wv"], "wg": inputs["L0_attn_wg"],
        "cosq": np.ascontiguousarray(cos).astype(BF),
        "sinq": np.ascontiguousarray(sin).astype(BF),
        "cosk": np.ascontiguousarray(cos * sc).astype(BF),
        "sink": np.ascontiguousarray(sin * sc).astype(BF),
        "costm": ctm8.astype(BF), "sintm": stm8.astype(BF),
        "rt": C["RT"], "ident": C["ident"],
    }


def run_A(inputs):
    nc = build_A()
    maps = [_inputs_A(inputs, c) for c in range(NCORES)]
    return run_bass_kernel_spmd(nc, maps, core_ids=list(range(NCORES))).results


def _io_B1(nc):
    io = {}
    io["xn_fm"] = nc.dram_tensor("xn_fm", [D_, TC], BF16, kind="ExternalInput").ap()
    io["wgT"] = nc.dram_tensor("wgT", [D_, HID0], I32, kind="ExternalInput").ap()
    io["wuT"] = nc.dram_tensor("wuT", [D_, HID0], I32, kind="ExternalInput").ap()
    io["hid"] = nc.dram_tensor("hid", [HID0, TC], BF16, kind="ExternalOutput").ap()
    return io


def emit_B1(nc, p, io):
    xnf_d = io["xn_fm"]
    wgT_d = io["wgT"]
    wuT_d = io["wuT"]
    hid_d = io["hid"]
    NH = HID0 // 128
    es_own = "es" not in io
    es = io.get("es") or ExitStack()
    try:
        xnf = [es.enter_context(nc.sbuf_tensor("B1." + f"xnf{i}", [128, TC], BF16)) for i in range(8)]
        wstg = [es.enter_context(nc.sbuf_tensor("B1." + f"wstg{i}", [128, 128], I32)) for i in range(4)]
        wgb = [es.enter_context(nc.sbuf_tensor("B1." + f"wgb{i}", [128, 8 * 128], BF16)) for i in range(2)]
        wub = [es.enter_context(nc.sbuf_tensor("B1." + f"wub{i}", [128, 8 * 128], BF16)) for i in range(2)]
        sgt = [es.enter_context(nc.sbuf_tensor("B1." + f"sgt{i}", [128, 512], BF16)) for i in range(2)]
        hres = [es.enter_context(nc.sbuf_tensor("B1." + f"hres{i}", [128, 512], BF16)) for i in range(2)]
        ps_g = [es.enter_context(nc.psum_tensor("B1." + "psg0", [128, 512], F32))] * 2
        ps_u = [es.enter_context(nc.psum_tensor("B1." + f"psu{i}", [128, 512], F32)) for i in range(2)]
        CV = CVals(nc, es, p, [1.0 / 127.0])

        ldxn = [p.dma("sync", lambda e, dc=dc: e.dma_start(out=xnf[dc][:], in_=xnf_d[dc * 128:(dc + 1) * 128, :]),
                      tag=f"ldxn{dc}", slot="f_init") for dc in range(8)]
        prev_wstg = [None] * 4
        prev_wb_user = [None, None]
        prev_sg_user = [None, None]
        prev_hres_user = [None, None]
        prev_psg_user = [None, None]
        prev_psu_user = [None, None]
        stores = []
        jj = 0
        for hc in range(NH):
            par = hc % 2
            convs = []
            for kc in range(8):
                for wi, (wd, wb) in enumerate(((wgT_d, wgb), (wuT_d, wub))):
                    j = jj
                    jj += 1
                    sl = j % 4
                    ldw = p.dma("sync",
                                lambda e, wd=wd, kc=kc, hc=hc, sl=sl:
                                e.dma_start(out=wstg[sl][:],
                                            in_=wd[kc * 128:(kc + 1) * 128, hc * 128:(hc + 1) * 128]),
                                deps=[prev_wstg[sl]], tag=f"ldw{hc}_{kc}_{wi}", slot=f"wstg{sl}")
                    cv_ = p.op("scalar",
                               lambda e, wb=wb, par=par, kc=kc, sl=sl:
                               e.activation(out=wb[par][:, kc * 128:(kc + 1) * 128],
                                            in_=wstg[sl][:], func=AF.Copy),
                               deps=[ldw, prev_wb_user[par]], tag=f"cvw{hc}_{kc}_{wi}")
                    prev_wstg[sl] = cv_
                    convs.append(cv_)
            for tb in range(4):
                s = tb * 512
                slot = tb % 2
                mg = []
                mu = []
                for kc in range(8):
                    mg.append(p.op("tensor",
                                   lambda e, par=par, kc=kc, s=s, slot=slot:
                                   e.matmul(out=ps_g[slot][:], lhsT=wgb[par][:, kc * 128:(kc + 1) * 128],
                                            rhs=xnf[kc][:, s:s + 512], start=(kc == 0), stop=(kc == 7)),
                                   deps=([convs[2 * kc], prev_psg_user[0], prev_psg_user[1]] + ldxn)
                                   if kc == 0 else [convs[2 * kc]],
                                   tag=f"mg{hc}_{tb}_{kc}"))
                for kc in range(8):
                    mu.append(p.op("tensor",
                                   lambda e, par=par, kc=kc, s=s, slot=slot:
                                   e.matmul(out=ps_u[slot][:], lhsT=wub[par][:, kc * 128:(kc + 1) * 128],
                                            rhs=xnf[kc][:, s:s + 512], start=(kc == 0), stop=(kc == 7)),
                                   deps=[prev_psu_user[slot], convs[2 * kc + 1]] if kc == 0 else [convs[2 * kc + 1]],
                                   tag=f"mu{hc}_{tb}_{kc}"))
                sg = p.op("scalar",
                          lambda e, slot=slot: e.activation(out=sgt[slot][:], in_=ps_g[slot][:],
                                                            func=AF.Silu, scale=CV.ap(1.0 / 127.0)),
                          deps=[mg[-1], CV.dep(1.0 / 127.0), prev_sg_user[slot]], tag=f"sg{hc}_{tb}")
                prev_psg_user[slot] = sg
                hd = p.op("vector",
                          lambda e, slot=slot: e.scalar_tensor_tensor(out=hres[slot][:], in0=ps_u[slot][:],
                                                                      scalar=CV.ap(1.0 / 127.0), in1=sgt[slot][:],
                                                                      op0=ALU.mult, op1=ALU.mult),
                          deps=[mu[-1], sg, CV.dep(1.0 / 127.0), prev_hres_user[slot]], tag=f"hd{hc}_{tb}")
                prev_psu_user[slot] = hd
                prev_sg_user[slot] = hd
                sto = p.dma("gpsimd",
                            lambda e, hc=hc, s=s, slot=slot:
                            e.dma_start(out=hid_d[hc * 128:(hc + 1) * 128, s:s + 512], in_=hres[slot][:]),
                            deps=[hd], tag=f"sth{hc}_{tb}", slot=f"hres{slot}")
                prev_hres_user[slot] = sto
                stores.append(sto)
            prev_wb_user[par] = mu[-1]
        p.barrier_op("gpsimd", stores, tag="final")
    finally:
        if es_own:
            es.close()


def build_B1():
    nc = bass.Bass("TRN2", target_bir_lowering=False, debug=False)
    io = _io_B1(nc)
    p = Prog(nc)
    p.slot_prefix = "B1."
    emit_B1(nc, p, io)
    with ExitStack() as es:
        block = es.enter_context(nc.Block())
        p.emit(es, block)
    return nc


def _io_B2(nc):
    io = {}
    io["hid"] = nc.dram_tensor("hid", [HID0, TC], BF16, kind="ExternalInput").ap()
    io["wdT"] = nc.dram_tensor("wdT", [HID0, D_], I32, kind="ExternalInput").ap()
    io["g0"] = nc.dram_tensor("g0", [TC, D_], BF16, kind="ExternalOutput").ap()
    return io


def emit_B2(nc, p, io):
    hid_d = io["hid"]
    wdT_d = io["wdT"]
    g0_d = io["g0"]
    NH = HID0 // 128
    NTC = TC // 128
    es_own = "es" not in io
    es = io.get("es") or ExitStack()
    try:
        wstg = [es.enter_context(nc.sbuf_tensor("B2." + f"wstg{i}", [128, D_], I32)) for i in range(2)]
        wdb = es.enter_context(nc.sbuf_tensor("B2." + "wdb", [128, NH * D_], BF16))
        hbuf = [es.enter_context(nc.sbuf_tensor("B2." + f"hbuf{i}", [128, NH * 128], BF16)) for i in range(2)]
        gres = [es.enter_context(nc.sbuf_tensor("B2." + f"gres{i}", [128, 512], BF16)) for i in range(2)]
        ps = [es.enter_context(nc.psum_tensor("B2." + "ps0", [128, 512], F32))] * 2
        CV = CVals(nc, es, p, [1.0 / 127.0])

        prev_conv = [None, None]
        convs = []
        for kc in range(NH):
            ldw = p.dma("sync",
                        lambda e, kc=kc: e.dma_start(out=wstg[kc % 2][:], in_=wdT_d[kc * 128:(kc + 1) * 128, :]),
                        deps=[prev_conv[kc % 2]], tag=f"ldw{kc}", slot=f"wstg{kc % 2}")
            cv_ = p.op("scalar",
                       lambda e, kc=kc: e.activation(out=wdb[:, kc * D_:(kc + 1) * D_],
                                                     in_=wstg[kc % 2][:], func=AF.Copy),
                       deps=[ldw], tag=f"cvw{kc}")
            prev_conv[kc % 2] = cv_
            convs.append(cv_)
        wready = convs[-1]

        prev_h_user = [None, None]
        prev_g_user = [None, None]
        prev_ps_user = [None, None]
        stores = []
        for tc in range(NTC):
            hpar = tc % 2
            lds = []
            for kc in range(NH):
                lds.append(p.dma("sync",
                                 lambda e, kc=kc, tc=tc, hpar=hpar:
                                 e.dma_start(out=hbuf[hpar][:].rearrange("p (k c) -> p k c", c=128)[:, kc, :],
                                             in_=hid_d[kc * 128:(kc + 1) * 128, tc * 128:(tc + 1) * 128])
                                 if False else
                                 e.dma_start(out=hbuf[hpar][:, kc * 128:(kc + 1) * 128],
                                             in_=hid_d[kc * 128:(kc + 1) * 128, tc * 128:(tc + 1) * 128],
                                             ),
                                 deps=[prev_h_user[hpar]] if kc == 0 else [], tag=f"ldh{tc}_{kc}",
                                 slot=f"hbuf{hpar}"))
            last_mm = None
            for ob in range(2):
                slot = ob
                mms = []
                for kc in range(NH):
                    mms.append(p.op("tensor",
                                    lambda e, kc=kc, ob=ob, hpar=hpar, slot=slot:
                                    e.matmul(out=ps[slot][:], lhsT=hbuf[hpar][:, kc * 128:(kc + 1) * 128],
                                             rhs=wdb[:, kc * D_ + ob * 512:kc * D_ + ob * 512 + 512],
                                             start=(kc == 0), stop=(kc == NH - 1)),
                                    deps=([wready, prev_ps_user[0], prev_ps_user[1]] + lds) if kc == 0 else [],
                                    tag=f"mm{tc}_{ob}_{kc}"))
                ev = p.op("scalar",
                          lambda e, slot=slot: e.activation(out=gres[slot][:], in_=ps[slot][:],
                                                            func=AF.Silu, scale=CV.ap(1.0 / 127.0)),
                          deps=[mms[-1], CV.dep(1.0 / 127.0), prev_g_user[slot]], tag=f"ev{tc}_{ob}")
                prev_ps_user[slot] = ev
                sto = p.dma("gpsimd",
                            lambda e, tc=tc, ob=ob, slot=slot:
                            e.dma_start(out=g0_d[tc * 128:(tc + 1) * 128, ob * 512:ob * 512 + 512],
                                        in_=gres[slot][:]),
                            deps=[ev], tag=f"stg{tc}_{ob}", slot=f"gres{slot}")
                prev_g_user[slot] = sto
                stores.append(sto)
                last_mm = mms[-1]
            prev_h_user[hpar] = last_mm
        p.barrier_op("gpsimd", stores, tag="final")
    finally:
        if es_own:
            es.close()


def build_B2():
    nc = bass.Bass("TRN2", target_bir_lowering=False, debug=False)
    io = _io_B2(nc)
    p = Prog(nc)
    p.slot_prefix = "B2."
    emit_B2(nc, p, io)
    with ExitStack() as es:
        block = es.enter_context(nc.Block())
        p.emit(es, block)
    return nc


def _io_C(nc):
    io = {}
    io["krtm"] = nc.dram_tensor("krtm", [TC, D_], BF16, kind="ExternalInput").ap()
    io["gktm"] = nc.dram_tensor("gktm", [128, H_], F32, kind="ExternalInput").ap()
    io["kr"] = nc.dram_tensor("kr", [D_, TC], BF16, kind="ExternalInput").ap()
    io["v"] = nc.dram_tensor("v", [TC, VD], BF16, kind="ExternalInput").ap()
    io["gk"] = nc.dram_tensor("gk", [128, H_ * CH], BF16, kind="ExternalInput").ap()
    io["ident"] = nc.dram_tensor("ident", [128, 128], BF16, kind="ExternalInput").ap()
    io["ones_col"] = nc.dram_tensor("ones_col", [128, 1], BF16, kind="ExternalInput").ap()
    io["t_out"] = nc.dram_tensor("t_out", [H_ * 128, HD + 1], F32, kind="ExternalOutput").ap()
    return io


def emit_C(nc, p, io):
    krtm_d = io["krtm"]
    gktm_d = io["gktm"]
    kr_d = io.get("kr")
    v_d = io["v"]
    gk_d = io["gk"]
    id_d = io["ident"]
    ones_d = io["ones_col"]
    t_d = io["t_out"]
    es_own = "es" not in io
    es = io.get("es") or ExitStack()
    try:
        krt = [es.enter_context(nc.sbuf_tensor("C." + f"krt{i}", [128, 128], BF16)) for i in range(2)]
        gktm = es.enter_context(nc.sbuf_tensor("C." + "gktm", [128, H_], F32))
        vt = [es.enter_context(nc.sbuf_tensor("C." + f"vt{i}", [128, VD], BF16)) for i in range(2)]
        gk = es.enter_context(nc.sbuf_tensor("C." + "sb_gk", [128, H_ * CH], BF16))
        ident = es.enter_context(nc.sbuf_tensor("C." + "s_ident", [128, 128], BF16))
        ones = es.enter_context(nc.sbuf_tensor("C." + "s_ones", [128, 1], BF16))
        Sb = [es.enter_context(nc.sbuf_tensor("C." + f"Sb{i}", [128, H_ * (HD + 1)], F32)) for i in range(2)]
        ktm = [es.enter_context(nc.sbuf_tensor("C." + f"ktm{i}", [128, 128], BF16)) for i in range(2)]
        ps_st = [es.enter_context(nc.psum_tensor("C." + f"psst{i}", [128, HD + 1], F32)) for i in range(2)]

        lds = []
        lds.append(p.dma("sync", lambda e: e.dma_start(out=gk[:], in_=gk_d), tag="ldgk", slot="f_init"))
        lds.append(p.dma("sync", lambda e: e.dma_start(out=gktm[:], in_=gktm_d), tag="ldgktm", slot="f_init"))
        lds.append(p.dma("sync", lambda e: e.dma_start(out=ident[:], in_=id_d), tag="ldid", slot="f_init"))
        lds.append(p.dma("sync", lambda e: e.dma_start(out=ones[:], in_=ones_d), tag="ldones", slot="f_init"))
        z0 = p.op("vector", lambda e: e.memset(Sb[0][:], 0.0), tag="zeroS")

        C = _consts()
        gC = C["gamma"] ** CH
        prev = {}
        upd_ops = {}
        for c in range(NCHUNK):
            vpar = c % 2
            ldv = p.dma("sync", lambda e, c=c, vpar=vpar:
                        e.dma_start(out=vt[vpar][:], in_=v_d[c * 128:(c + 1) * 128, :]),
                        deps=[prev.get(("vuse", vpar))], tag=f"ldv{c}", slot=f"vt{vpar}")
            for h in range(H_):
                i2 = c * H_ + h
                sl = i2 % 2
                ldk = p.dma("sync",
                            lambda e, c=c, h=h, sl=sl:
                            e.dma_start(out=krt[sl][:], in_=krtm_d[c * 128:(c + 1) * 128, h * 128:(h + 1) * 128]),
                            deps=[prev.get(("kruse", sl))], tag=f"ldkr{c}_{h}", slot=f"krt{sl}")
                kt = p.op("vector",
                          lambda e, h=h, sl=sl:
                          e.tensor_scalar(out=ktm[sl][:], in0=krt[sl][:],
                                          scalar1=gktm[:, h:h + 1], scalar2=None, op0=ALU.mult),
                          deps=lds + [ldk, prev.get(("mm", sl))], tag=f"kt{c}_{h}")
                prev[("kruse", sl)] = kt
                mm1 = p.op("tensor",
                           lambda e, c=c, h=h, sl=sl, vpar=vpar:
                           e.matmul(out=ps_st[sl][:, 0:HD], lhsT=ktm[sl][:],
                                    rhs=vt[vpar][:, h * HD:(h + 1) * HD], start=True, stop=False),
                           deps=[kt, ldv, prev.get(("upd", sl))], tag=f"mm1_{c}_{h}")
                mm2 = p.op("tensor",
                           lambda e, sl=sl: e.matmul(out=ps_st[sl][:, HD:HD + 1], lhsT=ktm[sl][:],
                                                     rhs=ones[:], start=False, stop=True),
                           deps=[], tag=f"mm2_{c}_{h}")
                prev[("mm", sl)] = mm2
                upd = p.op("vector",
                           lambda e, c=c, h=h, sl=sl, g=float(gC[h]):
                           e.scalar_tensor_tensor(out=Sb[(c + 1) % 2][:, h * (HD + 1):(h + 1) * (HD + 1)],
                                                  in0=Sb[c % 2][:, h * (HD + 1):(h + 1) * (HD + 1)],
                                                  scalar=g, in1=ps_st[sl][:],
                                                  op0=ALU.mult, op1=ALU.add),
                           deps=[mm2, z0, upd_ops.get((c - 1, h)), upd_ops.get((c - 2, h))],
                           tag=f"upd{c}_{h}")
                prev[("upd", sl)] = upd
                prev[("vuse", vpar)] = mm1
                upd_ops[(c, h)] = upd
        stores = []
        fin = NCHUNK % 2
        for h in range(H_):
            stores.append(p.dma("gpsimd",
                                lambda e, h=h, fin=fin:
                                e.dma_start(out=t_d[h * 128:(h + 1) * 128, :],
                                            in_=Sb[fin][:, h * (HD + 1):(h + 1) * (HD + 1)]),
                                deps=[upd_ops[(NCHUNK - 1, h)]], tag=f"stT{h}", slot="f_st"))
        p.barrier_op("gpsimd", stores, tag="final")
    finally:
        if es_own:
            es.close()


def build_C():
    nc = bass.Bass("TRN2", target_bir_lowering=False, debug=False)
    io = _io_C(nc)
    p = Prog(nc)
    p.slot_prefix = "C."
    emit_C(nc, p, io)
    with ExitStack() as es:
        block = es.enter_context(nc.Block())
        p.emit(es, block)
    return nc


def _io_D(nc):
    io = {}
    io["krtm"] = nc.dram_tensor("krtm", [TC, D_], BF16, kind="ExternalInput").ap()
    io["gktm"] = nc.dram_tensor("gktm", [128, H_], F32, kind="ExternalInput").ap()
    io["qr"] = nc.dram_tensor("qr", [D_, TC], BF16, kind="ExternalInput").ap()
    io["kr"] = nc.dram_tensor("kr", [D_, TC], BF16, kind="ExternalInput").ap()
    io["v"] = nc.dram_tensor("v", [TC, VD], BF16, kind="ExternalInput").ap()
    io["g"] = nc.dram_tensor("g", [TC, VD], BF16, kind="ExternalInput").ap()
    io["t_prev"] = nc.dram_tensor("t_prev", [128, H_ * 3 * (HD + 1)], F32, kind="ExternalInput").ap()
    io["gq"] = nc.dram_tensor("gq", [128, H_ * CH], BF16, kind="ExternalInput").ap()
    io["gk"] = nc.dram_tensor("gk", [128, H_ * CH], BF16, kind="ExternalInput").ap()
    io["dmT"] = nc.dram_tensor("dmT", [128, H_ * CH], BF16, kind="ExternalInput").ap()
    io["rden"] = nc.dram_tensor("rden", [128, NCHUNK * H_], F32, kind="ExternalInput").ap()
    io["ident"] = nc.dram_tensor("ident", [128, 128], BF16, kind="ExternalInput").ap()
    io["ones_col"] = nc.dram_tensor("ones_col", [128, 1], BF16, kind="ExternalInput").ap()
    io["wo"] = nc.dram_tensor("wo", [VD, D_], F32, kind="ExternalInput").ap()
    io["s0"] = nc.dram_tensor("s0", [TC, D_], BF16, kind="ExternalOutput").ap()
    return io


def emit_D(nc, p, io):
    qr_d = io["qr"]
    kr_d = io["kr"]
    v_d = io["v"]
    g_d = io["g"]
    tp_d = io.get("t_prev")
    krtm_d = io["krtm"]
    gktm_d = io["gktm"]
    gq_d = io["gq"]
    gk_d = io["gk"]
    dm_d = io["dmT"]
    rden_d = io["rden"]
    id_d = io["ident"]
    ones_d = io["ones_col"]
    wo_d = io["wo"]
    s0_d = io["s0"]
    es_own = "es" not in io
    es = io.get("es") or ExitStack()
    try:
        qrt = [es.enter_context(nc.sbuf_tensor("D." + f"qrt{i}", [128, 128], BF16)) for i in range(2)]
        krt2 = [es.enter_context(nc.sbuf_tensor("D." + f"krt{i}", [128, 128], BF16)) for i in range(2)]
        krtmt = [es.enter_context(nc.sbuf_tensor("D." + f"krtmt{i}", [128, 128], BF16)) for i in range(2)]
        gktm = es.enter_context(nc.sbuf_tensor("D." + "gktm", [128, H_], F32))
        vt = [es.enter_context(nc.sbuf_tensor("D." + f"vt{i}", [128, VD], BF16)) for i in range(2)]
        gt = [es.enter_context(nc.sbuf_tensor("D." + f"gt{i}", [128, VD], BF16)) for i in range(2)]
        gq = es.enter_context(nc.sbuf_tensor("D." + "sb_gq", [128, H_ * CH], BF16))
        gk = es.enter_context(nc.sbuf_tensor("D." + "sb_gk", [128, H_ * CH], BF16))
        dm = es.enter_context(nc.sbuf_tensor("D." + "sb_dm", [128, H_ * CH], BF16))
        rden = es.enter_context(nc.sbuf_tensor("D." + "sb_rden", [128, NCHUNK * H_], F32))
        ident = es.enter_context(nc.sbuf_tensor("D." + "s_ident", [128, 128], BF16))
        ones = es.enter_context(nc.sbuf_tensor("D." + "s_ones", [128, 1], BF16))
        tp = es.enter_context(nc.sbuf_tensor("D." + "sb_tp", [128, 4 * (HD + 1)], F32))
        coefsb = es.enter_context(nc.sbuf_tensor("D." + "sb_coefD", [128, 4 * H_], F32))
        Sb = [es.enter_context(nc.sbuf_tensor("D." + f"Sb{i}", [128, H_ * (HD + 1)], F32)) for i in range(2)]
        Sbf = [es.enter_context(nc.sbuf_tensor("D." + f"Sbf{i}", [128, H_ * (HD + 1)], BF16)) for i in range(2)]
        wob = es.enter_context(nc.sbuf_tensor("D." + "wob", [128, 16 * D_], BF16))
        wstg = [es.enter_context(nc.sbuf_tensor("D." + f"wstg{i}", [128, D_], F32)) for i in range(2)]
        ktil = [es.enter_context(nc.sbuf_tensor("D." + f"ktil{i}", [128, 128], BF16)) for i in range(2)]
        ktm = [es.enter_context(nc.sbuf_tensor("D." + f"ktm{i}", [128, 128], BF16)) for i in range(2)]
        qt_ = [es.enter_context(nc.sbuf_tensor("D." + f"qt{i}", [128, 128], BF16)) for i in range(2)]
        pT = [es.enter_context(nc.sbuf_tensor("D." + f"pT{i}", [128, 128], BF16)) for i in range(2)]
        sgfm = es.enter_context(nc.sbuf_tensor("D." + "sgfm", [128, 16 * 128], BF16))
        Usb = es.enter_context(nc.sbuf_tensor("D." + "Usb", [128, H_ * (HD + 1)], F32))
        Un = es.enter_context(nc.sbuf_tensor("D." + "Un", [128, H_ * (HD + 1)], F32))
        U2 = es.enter_context(nc.sbuf_tensor("D." + "U2", [128, VD], BF16))
        usq = es.enter_context(nc.sbuf_tensor("D." + "usq", [128, HD], BF16))
        sgl = [es.enter_context(nc.sbuf_tensor("D." + f"sgl{i}", [128, VD], BF16)) for i in range(2)]
        sguc = es.enter_context(nc.sbuf_tensor("D." + "sguc", [128, VD], BF16))
        sred = es.enter_context(nc.sbuf_tensor("D." + "sred", [128, 32], F32))
        s0t = [es.enter_context(nc.sbuf_tensor("D." + f"s0t{i}", [128, 512], BF16)) for i in range(2)]
        ps_s = [es.enter_context(nc.psum_tensor("D." + "pss0", [128, CH], F32))] * 2
        ps_u = [es.enter_context(nc.psum_tensor("D." + f"psu{i}", [128, HD + 1], F32)) for i in range(2)]
        ps_st = [es.enter_context(nc.psum_tensor("D." + f"psst{i}", [128, HD + 1], F32)) for i in range(2)]
        ps_tr = [es.enter_context(nc.psum_tensor("D." + "pstr0", [128, 128], BF16))] * 2
        ps_o = [es.enter_context(nc.psum_tensor("D." + "pso0", [128, 512], F32))] * 2
        CV = CVals(nc, es, p, [EPS, 1.0 / HD])

        C = _consts()
        gam = C["gamma"]
        lds = []
        for nm, sb, dd in (("gq", gq, gq_d), ("gk", gk, gk_d), ("dm", dm, dm_d)):
            lds.append(p.dma("sync", lambda e, sb=sb, dd=dd: e.dma_start(out=sb[:], in_=dd), tag=f"ld{nm}", slot="f_init"))
        lds.append(p.dma("sync", lambda e: e.dma_start(out=rden[:], in_=rden_d), tag="ldrden", slot="f_init"))
        lds.append(p.dma("sync", lambda e: e.dma_start(out=gktm[:], in_=gktm_d), tag="ldgktm", slot="f_init"))
        lds.append(p.dma("sync", lambda e: e.dma_start(out=ident[:], in_=id_d), tag="ldid", slot="f_init"))
        lds.append(p.dma("sync", lambda e: e.dma_start(out=ones[:], in_=ones_d), tag="ldones", slot="f_init"))

        prev_w = [None, None]
        wcv = []
        for kc in range(16):
            ldw = p.dma("sync", lambda e, kc=kc: e.dma_start(out=wstg[kc % 2][:], in_=wo_d[kc * 128:(kc + 1) * 128, :]),
                        deps=[prev_w[kc % 2]], tag=f"ldwo{kc}", slot=f"wstg{kc % 2}")
            cv_ = p.op("scalar", lambda e, kc=kc: e.activation(out=wob[:, kc * D_:(kc + 1) * D_],
                                                               in_=wstg[kc % 2][:], func=AF.Copy),
                       deps=[ldw], tag=f"cvwo{kc}")
            prev_w[kc % 2] = cv_
            wcv.append(cv_)
        NS = HD + 1
        s_init = []
        prev_tpld = None
        if "t_cc" in io:
            tcc_d = io["t_cc"]
            coef_d = io["coefD"]
            ldcoef = p.dma("sync", lambda e: e.dma_start(out=coefsb[:], in_=coef_d),
                           tag="ldcoef", slot="f_init")
            ccdep = io.get("ccdep")
            for h in range(H_):
                lds4 = []
                for j in range(4):
                    lds4.append(p.dma("gpsimd" if ccdep is not None else "sync",
                                      lambda e, h=h, j=j: e.dma_start(
                                          out=tp[:, j * NS:(j + 1) * NS],
                                          in_=tcc_d[j * H_ * 128 + h * 128:j * H_ * 128 + (h + 1) * 128, :]),
                                      deps=[prev_tpld, ccdep], tag=f"ldtp{h}_{j}", slot="tp"))
                a0 = p.op("vector",
                          lambda e, h=h: e.tensor_scalar(out=Sb[1][:, h * NS:(h + 1) * NS],
                                                         in0=tp[:, 0:NS],
                                                         scalar1=coefsb[:, h * 4:h * 4 + 1],
                                                         scalar2=None, op0=ALU.mult),
                          deps=lds4 + [ldcoef], tag=f"si0_{h}")
                aacc = a0
                for j in range(1, 4):
                    dst = Sb[1] if j % 2 == 0 else Sb[0]
                    srcb = Sb[0] if j % 2 == 0 else Sb[1]
                    aacc = p.op("vector",
                                lambda e, h=h, j=j, dst=dst, srcb=srcb: e.scalar_tensor_tensor(
                                    out=dst[:, h * NS:(h + 1) * NS],
                                    in0=tp[:, j * NS:(j + 1) * NS],
                                    scalar=coefsb[:, h * 4 + j:h * 4 + j + 1],
                                    in1=srcb[:, h * NS:(h + 1) * NS],
                                    op0=ALU.mult, op1=ALU.add),
                                deps=[aacc], tag=f"si{j}_{h}")
                prev_tpld = aacc
                bc = p.op("scalar",
                          lambda e, h=h: e.activation(out=Sbf[0][:, h * NS:(h + 1) * NS],
                                                      in_=Sb[0][:, h * NS:(h + 1) * NS], func=AF.Copy),
                          deps=[aacc], tag=f"sbf0_{h}")
                s_init.append((aacc, bc))
        else:
            for h in range(H_):
                g1 = float(gam[h] ** TC)
                g2 = float(gam[h] ** (2 * TC))
                NS3 = 3 * (HD + 1)
                ldtp = p.dma("sync",
                             lambda e, h=h, NS3=NS3: e.dma_start(
                                 out=tp[:, 0:NS3], in_=tp_d[:, h * NS3:(h + 1) * NS3]),
                             deps=[prev_tpld], tag=f"ldtp{h}", slot="tp")
                a1 = p.op("vector",
                          lambda e, h=h, g1=g1: e.scalar_tensor_tensor(
                              out=Sb[1][:, h * NS:(h + 1) * NS],
                              in0=tp[:, 1 * NS:2 * NS], scalar=g1,
                              in1=tp[:, 0 * NS:1 * NS], op0=ALU.mult, op1=ALU.add),
                          deps=[ldtp], tag=f"si1_{h}")
                a2 = p.op("vector",
                          lambda e, h=h, g2=g2: e.scalar_tensor_tensor(
                              out=Sb[0][:, h * NS:(h + 1) * NS],
                              in0=tp[:, 2 * NS:3 * NS], scalar=g2,
                              in1=Sb[1][:, h * NS:(h + 1) * NS], op0=ALU.mult, op1=ALU.add),
                          deps=[a1], tag=f"si2_{h}")
                prev_tpld = a2
                bc = p.op("scalar",
                          lambda e, h=h: e.activation(out=Sbf[0][:, h * NS:(h + 1) * NS],
                                                      in_=Sb[0][:, h * NS:(h + 1) * NS], func=AF.Copy),
                          deps=[a2], tag=f"sbf0_{h}")
                s_init.append((a2, bc))

        gC = gam ** CH
        prev = {}
        updf = {}
        mmu3_ops = {}
        stores = []
        for c in range(NCHUNK):
            cs = slice(c * CH, (c + 1) * CH)
            vpar = c % 2
            ldv = p.dma("sync", lambda e, c=c, vpar=vpar: e.dma_start(out=vt[vpar][:], in_=v_d[c * 128:(c + 1) * 128, :]),
                        deps=[prev.get(("vuse", vpar))], tag=f"ldv{c}", slot=f"vt{vpar}")
            ldg = p.dma("sync", lambda e, c=c, vpar=vpar: e.dma_start(out=gt[vpar][:], in_=g_d[c * 128:(c + 1) * 128, :]),
                        deps=[prev.get(("guse", vpar))], tag=f"ldg{c}", slot=f"gt{vpar}")
            sgchunk = p.op("scalar",
                           lambda e, vpar=vpar: e.activation(out=sgl[vpar][:], in_=gt[vpar][:], func=AF.Silu),
                           deps=[ldg, prev.get(("sgluse2", vpar))], tag=f"sgc{c}")
            tr_evs = []
            evus = []
            for h in range(H_):
                i2 = c * H_ + h
                sl = i2 % 2
                pupd, psbf = updf.get((c - 1, h), (None, None))
                if c == 0:
                    pupd, psbf = s_init[h]
                ldqr = p.dma("sync",
                             lambda e, c=c, h=h, sl=sl:
                             e.dma_start(out=qrt[sl][:], in_=qr_d[h * 128:(h + 1) * 128, c * CH:(c + 1) * CH]),
                             deps=[prev.get(("qruse", sl))], tag=f"ldqr{c}_{h}", slot=f"qrt{sl}")
                ldkr = p.dma("sync",
                             lambda e, c=c, h=h, sl=sl:
                             e.dma_start(out=krt2[sl][:], in_=kr_d[h * 128:(h + 1) * 128, c * CH:(c + 1) * CH]),
                             deps=[prev.get(("kruse", sl))], tag=f"ldkr{c}_{h}", slot=f"krt{sl}")
                mmS = p.op("tensor",
                           lambda e, sl=sl: e.matmul(out=ps_s[sl][:], lhsT=krt2[sl][:],
                                                     rhs=qrt[sl][:], start=True, stop=True),
                           deps=lds + [ldqr, ldkr, prev.get("pTdone")], tag=f"mmS{c}_{h}")
                pt = p.op("vector",
                          lambda e, h=h, sl=sl: e.tensor_tensor(out=pT[sl][:], in0=ps_s[sl][:],
                                                                in1=dm[:, h * CH:(h + 1) * CH], op=ALU.mult),
                          deps=[mmS, prev.get(("ptuse", sl))], tag=f"pt{c}_{h}")
                prev["pTdone"] = pt
                qtop = p.op("vector",
                            lambda e, h=h, sl=sl: e.tensor_tensor(out=qt_[sl][:], in0=qrt[sl][:],
                                                                  in1=gq[:, h * CH:(h + 1) * CH], op=ALU.mult),
                            deps=lds + [ldqr, prev.get(("qtuse", sl))], tag=f"qt{c}_{h}")
                usl = i2 % 2
                mmU1 = p.op("tensor",
                            lambda e, h=h, sl=sl, usl=usl, vpar=vpar:
                            e.matmul(out=ps_u[usl][:, 0:HD], lhsT=pT[sl][:],
                                     rhs=vt[vpar][:, h * HD:(h + 1) * HD], start=True, stop=False),
                            deps=[pt, ldv, prev.get(("uuse", usl))], tag=f"mmU1{c}_{h}")
                mmU2 = p.op("tensor",
                            lambda e, sl=sl, usl=usl: e.matmul(out=ps_u[usl][:, HD:HD + 1], lhsT=pT[sl][:],
                                                               rhs=ones[:], start=False, stop=False),
                            deps=[], tag=f"mmU2{c}_{h}")
                mmU3 = p.op("tensor",
                            lambda e, h=h, sl=sl, usl=usl, spar=c % 2:
                            e.matmul(out=ps_u[usl][:], lhsT=qt_[sl][:],
                                     rhs=Sbf[spar][:, h * NS:(h + 1) * NS], start=False, stop=True),
                            deps=[qtop, psbf], tag=f"mmU3{c}_{h}")
                mmu3_ops[(c, h)] = mmU3
                prev[("ptuse", sl)] = mmU2
                prev[("qtuse", sl)] = mmU3
                prev[("qruse", sl)] = qtop
                ldktm = p.dma("sync",
                              lambda e, c=c, h=h, sl=sl:
                              e.dma_start(out=krtmt[sl][:],
                                          in_=krtm_d[c * 128:(c + 1) * 128, h * 128:(h + 1) * 128]),
                              deps=[prev.get(("krtmuse", sl))], tag=f"ldktm{c}_{h}", slot=f"krtmt{sl}")
                kt = p.op("vector",
                          lambda e, h=h, sl=sl: e.tensor_scalar(out=ktm[sl][:], in0=krtmt[sl][:],
                                                                scalar1=gktm[:, h:h + 1], scalar2=None,
                                                                op0=ALU.mult),
                          deps=lds + [ldktm, prev.get(("ktmuse", sl))], tag=f"kt{c}_{h}")
                prev[("krtmuse", sl)] = kt
                mmS1 = p.op("tensor",
                            lambda e, h=h, sl=sl, vpar=vpar, ssl=i2 % 2:
                            e.matmul(out=ps_st[ssl][:, 0:HD], lhsT=ktm[sl][:],
                                     rhs=vt[vpar][:, h * HD:(h + 1) * HD],
                                     start=True, stop=False),
                            deps=[kt, prev.get(("stuse", i2 % 2))], tag=f"mmS1{c}_{h}")
                mmS2 = p.op("tensor",
                            lambda e, sl=sl, ssl=i2 % 2: e.matmul(out=ps_st[ssl][:, HD:HD + 1], lhsT=ktm[sl][:],
                                                                  rhs=ones[:], start=False, stop=True),
                            deps=[], tag=f"mmS2{c}_{h}")
                prev[("ktmuse", sl)] = mmS2
                upd = p.op("vector",
                           lambda e, h=h, c=c, g=float(gC[h]), ssl=i2 % 2:
                           e.scalar_tensor_tensor(out=Sb[(c + 1) % 2][:, h * NS:(h + 1) * NS],
                                                  in0=Sb[c % 2][:, h * NS:(h + 1) * NS], scalar=g,
                                                  in1=ps_st[ssl][:], op0=ALU.mult, op1=ALU.add),
                           deps=[mmS2, pupd,
                                 updf.get((c - 2, h), (None, None))[0] if c >= 2 else None,
                                 updf.get((c - 2, h), (None, None))[1] if c >= 2 else None],
                           tag=f"upd{c}_{h}")
                prev[("stuse", i2 % 2)] = upd
                sbf = p.op("scalar",
                           lambda e, h=h, c=c: e.activation(out=Sbf[(c + 1) % 2][:, h * NS:(h + 1) * NS],
                                                            in_=Sb[(c + 1) % 2][:, h * NS:(h + 1) * NS],
                                                            func=AF.Copy),
                           deps=[upd,
                                 updf.get((c - 1, h), (None, None))[1] if c >= 1 else None,
                                 mmu3_ops.get((c - 1, h))],
                           tag=f"sbf{c}_{h}")
                updf[(c, h)] = (upd, sbf)
                evu = p.op("vector",
                           lambda e, usl=usl, h=h: e.tensor_copy(
                               out=Usb[:, h * NS:(h + 1) * NS], in_=ps_u[usl][:]),
                           deps=[mmU3, prev.get("unb"), prev.get(("uuse", usl))], tag=f"evu{c}_{h}")
                prev[("uuse", usl)] = evu
                evus.append(evu)
                prev[("vuse", vpar)] = mmS1
                prev[("guse", vpar)] = sgchunk
            azb = p.op("scalar",
                       lambda e: e.activation(out=sred[:, 0:8],
                                              in_=bass.AP(Usb, HD, [[H_ * NS, 128], [NS, H_]]),
                                              func=AF.Abs),
                       deps=evus + [prev.get("dnb"), prev.get("u2b")], tag=f"azb{c}")
            dnb = p.op("vector",
                       lambda e, c=c: e.tensor_tensor(out=sred[:, 8:16], in0=sred[:, 0:8],
                                                      in1=rden[:, c * H_:(c + 1) * H_], op=ALU.max),
                       deps=[azb, lds[3]], tag=f"dnb{c}")
            prev["dnb"] = dnb
            rb1 = p.op("vector", lambda e: e.reciprocal(out=sred[:, 16:24], in_=sred[:, 8:16]),
                       deps=[dnb], tag=f"rb1{c}")
            unb = p.op("vector",
                       lambda e: e.tensor_tensor(
                           out=Un[:].rearrange("p (h d) -> p h d", h=H_),
                           in0=Usb[:].rearrange("p (h d) -> p h d", h=H_),
                           in1=bass.AP(sred, 16, [[32, 128], [1, H_], [0, NS]]),
                           op=ALU.mult),
                       deps=[rb1, prev.get("u2b")], tag=f"unb{c}")
            prev["unb"] = unb
            sqops = []
            for h in range(H_):
                sqops.append(p.op("scalar",
                                  lambda e, h=h: e.activation(out=usq[:], in_=Un[:, h * NS:h * NS + HD],
                                                              func=AF.Square, accum_out=sred[:, 24 + h:25 + h]),
                                  deps=[unb] + ([sqops[-1]] if sqops else [prev.get("sq3b")]), tag=f"sqb{c}_{h}"))
            sq3b = p.op("scalar",
                        lambda e: e.activation(out=sred[:, 0:8], in_=sred[:, 24:32], func=AF.Sqrt,
                                               scale=CV.ap(1.0 / HD), bias=CV.ap(EPS)),
                        deps=[sqops[-1], CV.dep(1.0 / HD), CV.dep(EPS), azb], tag=f"sq3b{c}")
            prev["sq3b"] = sq3b
            rb2 = p.op("vector", lambda e: e.reciprocal(out=sred[:, 8:16], in_=sred[:, 0:8]),
                       deps=[sq3b], tag=f"rb2{c}")
            u2b = p.op("vector",
                       lambda e: e.tensor_tensor(
                           out=U2[:].rearrange("p (h d) -> p h d", h=H_),
                           in0=bass.AP(Un, 0, [[H_ * NS, 128], [NS, H_], [1, HD]]),
                           in1=bass.AP(sred, 8, [[32, 128], [1, H_], [0, HD]]),
                           op=ALU.mult),
                       deps=[rb2, prev.get("sgub")], tag=f"u2b{c}")
            prev["u2b"] = u2b
            sgub = p.op("vector",
                        lambda e, vpar=vpar: e.tensor_tensor(out=sguc[:], in0=U2[:],
                                                             in1=sgl[vpar][:], op=ALU.mult),
                        deps=[u2b, sgchunk, prev.get("trevlast")], tag=f"sgub{c}")
            prev["sgub"] = sgub
            prev[("sgluse2", vpar)] = sgub
            for h in range(H_):
                for half in range(2):
                    j = h * 2 + half
                    trh = p.op("tensor",
                               lambda e, j=j: e.transpose(out=ps_tr[1][:],
                                                          in_=sguc[:, j * 128:(j + 1) * 128],
                                                          identity=ident[:]),
                               deps=[sgub, prev.get("trev")], tag=f"trsg{c}_{j}")
                    evh = p.op("vector",
                               lambda e, j=j: e.tensor_copy(
                                   out=sgfm[:, j * 128:(j + 1) * 128], in_=ps_tr[1][:]),
                               deps=[trh, prev.get(("sgfmuse", j))], tag=f"evsg{c}_{j}")
                    prev["trev"] = evh
                    tr_evs.append(evh)
            prev["trevlast"] = tr_evs[-1]
            for ob in range(2):
                osl = ob
                mos = []
                for kc in range(16):
                    mos.append(p.op("tensor",
                                    lambda e, kc=kc, ob=ob, osl=osl:
                                    e.matmul(out=ps_o[osl][:], lhsT=sgfm[:, kc * 128:(kc + 1) * 128],
                                             rhs=wob[:, kc * D_ + ob * 512:kc * D_ + ob * 512 + 512],
                                             start=(kc == 0), stop=(kc == 15)),
                                    deps=(tr_evs + [wcv[-1], prev.get("pouse")]) if kc == 0 else [],
                                    tag=f"mo{c}_{ob}_{kc}"))
                evo = p.op("vector",
                           lambda e, osl=osl: e.tensor_copy(out=s0t[osl][:], in_=ps_o[osl][:]),
                           deps=[mos[-1], prev.get(("s0use", osl))], tag=f"evo{c}_{ob}")
                prev["pouse"] = evo
                sto = p.dma("gpsimd",
                            lambda e, c=c, ob=ob, osl=osl:
                            e.dma_start(out=s0_d[c * 128:(c + 1) * 128, ob * 512:ob * 512 + 512],
                                        in_=s0t[osl][:]),
                            deps=[evo], tag=f"sts0{c}_{ob}", slot=f"s0t{osl}")
                prev[("s0use", osl)] = sto
                stores.append(sto)
            for j in range(16):
                prev[("sgfmuse", j)] = mos[-1]
        p.barrier_op("gpsimd", stores, tag="final")
    finally:
        if es_own:
            es.close()


def build_D():
    nc = bass.Bass("TRN2", target_bir_lowering=False, debug=False)
    io = _io_D(nc)
    p = Prog(nc)
    p.slot_prefix = "D."
    emit_D(nc, p, io)
    with ExitStack() as es:
        block = es.enter_context(nc.Block())
        p.emit(es, block)
    return nc


def _io_E(nc):
    io = {}
    io["x"] = nc.dram_tensor("x", [TC, D_], F32, kind="ExternalInput").ap()
    io["s0"] = nc.dram_tensor("s0", [TC, D_], BF16, kind="ExternalInput").ap()
    io["g0"] = nc.dram_tensor("g0", [TC, D_], BF16, kind="ExternalInput").ap()
    io["nw"] = nc.dram_tensor("nw", [128, 8], F32, kind="ExternalInput").ap()
    io["ident"] = nc.dram_tensor("ident", [128, 128], BF16, kind="ExternalInput").ap()
    io["x1"] = nc.dram_tensor("x1", [TC, D_], F32, kind="ExternalOutput").ap()
    io["xn1_tm"] = nc.dram_tensor("xn1_tm", [TC, D_], BF16, kind="ExternalOutput").ap()
    io["xn1_fm"] = nc.dram_tensor("xn1_fm", [D_, TC], BF16, kind="ExternalOutput").ap()
    io["xn2_fm"] = nc.dram_tensor("xn2_fm", [D_, TC], BF16, kind="ExternalOutput").ap()
    return io


def emit_E(nc, p, io):
    x_d = io["x"]
    s0_d = io["s0"]
    g0_d = io["g0"]
    nw_d = io["nw"]
    id_d = io["ident"]
    x1_d = io["x1"]
    xn1tm_d = io["xn1_tm"]
    xn1f_d = io["xn1_fm"]
    xn2f_d = io["xn2_fm"]
    NTC = TC // 128
    es_own = "es" not in io
    es = io.get("es") or ExitStack()
    try:
        xt = [es.enter_context(nc.sbuf_tensor("E." + f"xt{i}", [128, D_], F32)) for i in range(2)]
        st_ = [es.enter_context(nc.sbuf_tensor("E." + f"st{i}", [128, D_], BF16)) for i in range(2)]
        gt_ = [es.enter_context(nc.sbuf_tensor("E." + f"gt{i}", [128, D_], BF16)) for i in range(2)]
        sg = es.enter_context(nc.sbuf_tensor("E." + "sg", [128, D_], BF16))
        x1 = [es.enter_context(nc.sbuf_tensor("E." + f"x1_{i}", [128, D_], F32)) for i in range(2)]
        xsq = es.enter_context(nc.sbuf_tensor("E." + "xsq", [128, D_], BF16))
        red = es.enter_context(nc.sbuf_tensor("E." + "red", [128, 16], F32))
        xn1 = [es.enter_context(nc.sbuf_tensor("E." + f"xn1_{i}", [128, D_], BF16)) for i in range(2)]
        xnr = [es.enter_context(nc.sbuf_tensor("E." + f"xnr{i}", [128, D_], BF16)) for i in range(2)]
        nw = es.enter_context(nc.sbuf_tensor("E." + "sb_nw", [128, 8], F32))
        ident = es.enter_context(nc.sbuf_tensor("E." + "s_ident", [128, 128], BF16))
        xn1f = [es.enter_context(nc.sbuf_tensor("E." + f"xn1f{i}", [128, TC], BF16)) for i in range(8)]
        xn2f = [es.enter_context(nc.sbuf_tensor("E." + f"xn2f{i}", [128, TC], BF16)) for i in range(8)]
        ps_tr = [es.enter_context(nc.psum_tensor("E." + f"pstr{i}", [128, 128], BF16)) for i in range(4)]
        CV = CVals(nc, es, p, [EPS])

        ldid = p.dma("sync", lambda e: e.dma_start(out=ident[:], in_=id_d), tag="ldid", slot="f_init")
        ldnw = p.dma("sync", lambda e: e.dma_start(out=nw[:], in_=nw_d), tag="ldnw", slot="f_init")
        stores = []
        prev_ld = [None, None]
        prev_sq = None
        prev_region = [(None, None), (None, None)]
        prev_tr = {}
        prev_x1_store = [None, None]
        prev_xn_store = [None, None]
        last_evs = []
        for i in range(NTC):
            par = i % 2
            rb = par * 8

            def rc(j, rb=rb):
                return red[:, rb + j:rb + j + 1]

            pxn, psqr = prev_region[par]
            ldx = p.dma("sync", lambda e, i=i, par=par: e.dma_start(out=xt[par][:], in_=x_d[i * 128:(i + 1) * 128, :]),
                        deps=[prev_ld[par]], tag=f"ldx{i}", slot=f"ext{par}")
            ldst = p.dma("sync", lambda e, i=i, par=par: e.dma_start(out=st_[par][:], in_=s0_d[i * 128:(i + 1) * 128, :]),
                         deps=[prev_ld[par]], tag=f"lds{i}", slot=f"est{par}")
            ldgt = p.dma("sync", lambda e, i=i, par=par: e.dma_start(out=gt_[par][:], in_=g0_d[i * 128:(i + 1) * 128, :]),
                         deps=[prev_ld[par]], tag=f"ldg{i}", slot=f"egt{par}")
            sgo = p.op("vector", lambda e, par=par: e.tensor_tensor(out=sg[:], in0=st_[par][:], in1=gt_[par][:], op=ALU.mult),
                       deps=[ldst, ldgt, prev_sq], tag=f"sg{i}")
            x1o = p.op("vector", lambda e, par=par: e.tensor_tensor(out=x1[par][:], in0=xt[par][:], in1=sg[:], op=ALU.add),
                       deps=[sgo, ldx, prev_ld[par], prev_x1_store[par]], tag=f"x1_{i}")
            stx1 = p.dma("gpsimd", lambda e, i=i, par=par: e.dma_start(out=x1_d[i * 128:(i + 1) * 128, :], in_=x1[par][:]),
                         deps=[x1o], tag=f"stx1{i}", slot=f"x1st{par}")
            prev_x1_store[par] = stx1
            stores.append(stx1)
            sq = p.op("scalar",
                      lambda e, par=par, rc=rc: e.activation(out=xsq[:], in_=x1[par][:], func=AF.Square,
                                                             accum_out=rc(1)),
                      deps=[x1o, prev_sq, pxn], tag=f"sq{i}")
            prev_sq = sq
            rs = p.op("vector", lambda e, par=par, rc=rc: e.tensor_reduce(out=rc(0), in_=x1[par][:], axis=AX.X, op=ALU.add),
                      deps=[x1o, pxn, psqr], tag=f"rs{i}")
            m_ = p.op("vector", lambda e, rc=rc: e.tensor_scalar(out=rc(2), in0=rc(0), scalar1=1.0 / D_, scalar2=None, op0=ALU.mult),
                      deps=[rs], tag=f"m{i}")
            m2 = p.op("vector", lambda e, rc=rc: e.tensor_tensor(out=rc(5), in0=rc(2), in1=rc(2), op=ALU.mult),
                      deps=[m_], tag=f"m2_{i}")
            var = p.op("vector", lambda e, rc=rc: e.scalar_tensor_tensor(out=rc(3), in0=rc(1), scalar=1.0 / D_, in1=rc(5),
                                                                         op0=ALU.mult, op1=ALU.subtract),
                       deps=[m2, sq], tag=f"var{i}")
            sqr = p.op("scalar", lambda e, rc=rc: e.activation(out=rc(4), in_=rc(3), func=AF.Sqrt, bias=CV.ap(EPS)),
                       deps=[var, CV.dep(EPS)], tag=f"sqrt{i}")
            rcp = p.op("vector", lambda e, rc=rc: e.reciprocal(out=rc(6), in_=rc(4)), deps=[sqr], tag=f"rcp{i}")
            msr = p.op("vector", lambda e, rc=rc: e.tensor_tensor(out=rc(7), in0=rc(2), in1=rc(6), op=ALU.mult),
                       deps=[rcp], tag=f"msr{i}")
            xn = p.op("vector",
                      lambda e, par=par, rc=rc: e.tensor_scalar(out=xn1[par][:], in0=x1[par][:], scalar1=rc(6), scalar2=rc(7),
                                                                op0=ALU.mult, op1=ALU.subtract),
                      deps=[msr, prev_tr.get(("xn1use", par)), prev_xn_store[par]], tag=f"xn{i}")
            stxn = p.dma("gpsimd", lambda e, i=i, par=par: e.dma_start(out=xn1tm_d[i * 128:(i + 1) * 128, :], in_=xn1[par][:]),
                         deps=[xn], tag=f"stxn{i}", slot=f"xnst{par}")
            prev_xn_store[par] = stxn
            stores.append(stxn)
            prev_ld[par] = xn
            sq2 = p.op("scalar",
                       lambda e, par=par, rc=rc: e.activation(out=xsq[:], in_=xn1[par][:], func=AF.Square,
                                                              accum_out=rc(1)),
                       deps=[xn, var, prev_sq], tag=f"sq2_{i}")
            prev_sq = sq2
            v2 = p.op("vector", lambda e, rc=rc: e.tensor_scalar(out=rc(3), in0=rc(1), scalar1=1.0 / D_, scalar2=None, op0=ALU.mult),
                      deps=[sq2, sqr], tag=f"v2_{i}")
            sq3 = p.op("scalar", lambda e, rc=rc: e.activation(out=rc(4), in_=rc(3), func=AF.Sqrt, bias=CV.ap(EPS)),
                       deps=[v2, rcp], tag=f"sq3_{i}")
            rcp2 = p.op("vector", lambda e, rc=rc: e.reciprocal(out=rc(6), in_=rc(4)),
                        deps=[sq3, msr, xn], tag=f"rcp2_{i}")
            xnro = p.op("vector",
                        lambda e, par=par, rc=rc: e.tensor_scalar(out=xnr[par][:], in0=xn1[par][:], scalar1=rc(6),
                                                                  scalar2=None, op0=ALU.mult),
                        deps=[rcp2, prev_tr.get(("xnruse", par))], tag=f"xnr{i}")
            prev_region[par] = (xnro, sq3)
            evs = []
            for dc in range(8):
                sl1 = dc % 2
                tr1 = p.op("tensor",
                           lambda e, par=par, dc=dc, sl1=sl1: e.transpose(out=ps_tr[sl1][:],
                                                                          in_=xn1[par][:, dc * 128:(dc + 1) * 128],
                                                                          identity=ident[:]),
                           deps=[xn, ldid, prev_tr.get(("ps", sl1))], tag=f"t1_{i}_{dc}")
                ev1 = p.op("vector",
                           lambda e, i=i, dc=dc, sl1=sl1: e.tensor_copy(out=xn1f[dc][:, i * 128:(i + 1) * 128],
                                                                        in_=ps_tr[sl1][:]),
                           deps=[tr1], tag=f"e1_{i}_{dc}")
                prev_tr[("ps", sl1)] = ev1
                sl2 = 2 + dc % 2
                tr2 = p.op("tensor",
                           lambda e, par=par, dc=dc, sl2=sl2: e.transpose(out=ps_tr[sl2][:],
                                                                          in_=xnr[par][:, dc * 128:(dc + 1) * 128],
                                                                          identity=ident[:]),
                           deps=[xnro, ldid, prev_tr.get(("ps", sl2))], tag=f"t2_{i}_{dc}")
                ev2 = p.op("vector",
                           lambda e, i=i, dc=dc, sl2=sl2: e.tensor_scalar(out=xn2f[dc][:, i * 128:(i + 1) * 128],
                                                                          in0=ps_tr[sl2][:],
                                                                          scalar1=nw[:, dc:dc + 1], scalar2=None,
                                                                          op0=ALU.mult),
                           deps=[tr2, ldnw], tag=f"e2_{i}_{dc}")
                prev_tr[("ps", sl2)] = ev2
                evs += [ev1, ev2]
            prev_tr[("xn1use", par)] = evs[-2]
            prev_tr[("xnruse", par)] = evs[-1]
            last_evs = evs
        for dc in range(8):
            stores.append(p.dma("gpsimd", lambda e, dc=dc: e.dma_start(out=xn1f_d[dc * 128:(dc + 1) * 128, :], in_=xn1f[dc][:]),
                                deps=last_evs, tag=f"stf1{dc}", slot="f_st"))
            stores.append(p.dma("gpsimd", lambda e, dc=dc: e.dma_start(out=xn2f_d[dc * 128:(dc + 1) * 128, :], in_=xn2f[dc][:]),
                                deps=last_evs, tag=f"stf2{dc}", slot="f_st"))
        p.barrier_op("gpsimd", stores, tag="final")
    finally:
        if es_own:
            es.close()


def build_E():
    nc = bass.Bass("TRN2", target_bir_lowering=False, debug=False)
    io = _io_E(nc)
    p = Prog(nc)
    p.set_stage("E.")
    emit_E(nc, p, io)
    with ExitStack() as es:
        block = es.enter_context(nc.Block())
        p.emit(es, block)
    return nc


def _io_F(nc):
    io = {}
    io["xn2f"] = nc.dram_tensor("xn2f", [D_, S_], BF16, kind="ExternalInput").ap()
    io["inwT"] = nc.dram_tensor("inwT", [D_, ESL], F32, kind="ExternalInput").ap()
    io["cw"] = nc.dram_tensor("cw", [128, 16], F32, kind="ExternalInput").ap()
    io["cb"] = nc.dram_tensor("cb", [128, 4], F32, kind="ExternalInput").ap()
    io["bcdw"] = nc.dram_tensor("bcdw", [ESL, 96], F32, kind="ExternalInput").ap()
    io["xc"] = nc.dram_tensor("xc", [ESL, S_], BF16, kind="ExternalOutput").ap()
    io["bcdp"] = nc.dram_tensor("bcdp", [96, S_], F32, kind="ExternalOutput").ap()
    return io


def emit_F(nc, p, io):
    ag_mode = "xn2_is_ag" in io
    xn2_d = io["xn2f"]
    inwT_d = io["inwT"]
    cw_d = io["cw"]
    cb_d = io["cb"]
    bcdw_d = io["bcdw"]
    xc_d = io["xc"]
    bcdp_d = io["bcdp"]
    NTB = S_ // 512
    es_own = "es" not in io
    es = io.get("es") or ExitStack()
    try:
        inwb = es.enter_context(nc.sbuf_tensor("F." + "inwb", [128, 8 * ESL], BF16))
        wstg = [es.enter_context(nc.sbuf_tensor("F." + f"wstg{i}", [128, ESL], F32)) for i in range(2)]
        bcdb = es.enter_context(nc.sbuf_tensor("F." + "bcdb", [128, 4 * 96], BF16))
        bstg = es.enter_context(nc.sbuf_tensor("F." + "bstg", [128, 96], F32))
        cw = es.enter_context(nc.sbuf_tensor("F." + "sb_cw", [128, 16], F32))
        cb = es.enter_context(nc.sbuf_tensor("F." + "sb_cb", [128, 4], F32))
        xnstg = [es.enter_context(nc.sbuf_tensor("F." + f"xnstg{i}", [128, 8 * 512], BF16)) for i in range(2)]
        xin = [es.enter_context(nc.sbuf_tensor("F." + f"xin{i}", [128, 3 + S_], BF16)) for i in range(2)]
        acc = [es.enter_context(nc.sbuf_tensor("F." + f"acc{i}", [128, S_ // 4], F32)) for i in range(2)]
        xc = [es.enter_context(nc.sbuf_tensor("F." + f"xc{i}", [128, S_], BF16)) for i in range(4)]
        bres = [es.enter_context(nc.sbuf_tensor("F." + f"bres{i}", [96, 512], F32)) for i in range(2)]
        ps = [es.enter_context(nc.psum_tensor("F." + f"ps{i}", [128, 512], F32)) for i in range(2)]
        ps_b = [es.enter_context(nc.psum_tensor("F." + f"psb{i}", [96, 512], F32)) for i in range(2)]

        prev_w = [None, None]
        convs = []
        for kc in range(8):
            ldw = p.dma("sync", lambda e, kc=kc: e.dma_start(out=wstg[kc % 2][:], in_=inwT_d[kc * 128:(kc + 1) * 128, :]),
                        deps=[prev_w[kc % 2]], tag=f"ldw{kc}", slot=f"wstg{kc % 2}")
            cv_ = p.op("scalar", lambda e, kc=kc: e.activation(out=inwb[:, kc * ESL:(kc + 1) * ESL],
                                                               in_=wstg[kc % 2][:], func=AF.Copy),
                       deps=[ldw], tag=f"cvw{kc}")
            prev_w[kc % 2] = cv_
            convs.append(cv_)
        bconvs = []
        prev_b = None
        for ec in range(4):
            ldb = p.dma("sync", lambda e, ec=ec: e.dma_start(out=bstg[:], in_=bcdw_d[ec * 128:(ec + 1) * 128, :]),
                        deps=[prev_b], tag=f"ldb{ec}", slot="bstg")
            cv_ = p.op("scalar", lambda e, ec=ec: e.activation(out=bcdb[:, ec * 96:(ec + 1) * 96],
                                                               in_=bstg[:], func=AF.Copy),
                       deps=[ldb], tag=f"cvb{ec}")
            prev_b = cv_
            bconvs.append(cv_)
        ldcw = p.dma("sync", lambda e: e.dma_start(out=cw[:], in_=cw_d), tag="ldcw", slot="f_init")
        ldcb = p.dma("sync", lambda e: e.dma_start(out=cb[:], in_=cb_d), tag="ldcb", slot="f_init")

        conv_done = {}
        xc_st = []
        prev_sil = None
        prev_xn = [None, None]
        prev_ps = [None, None]
        prev_xinuse = [None, None]
        for ec in range(4):
            epar = ec % 2
            z0 = p.op("vector", lambda e, epar=epar: e.memset(xin[epar][:, 0:3], 0.0),
                      deps=[prev_xinuse[epar]], tag=f"z{ec}")
            evs = []
            for tb in range(NTB):
                xpar = (ec * NTB + tb) % 2
                lds = []
                for kc in range(8):
                    if ag_mode:
                        src_ap = xn2_d[(tb // 4) * 1024 + kc * 128:(tb // 4) * 1024 + (kc + 1) * 128,
                                       (tb % 4) * 512:(tb % 4 + 1) * 512]
                    else:
                        src_ap = xn2_d[kc * 128:(kc + 1) * 128, tb * 512:(tb + 1) * 512]
                    lds.append(p.dma("sync",
                                     lambda e, kc=kc, xpar=xpar, src_ap=src_ap:
                                     e.dma_start(out=xnstg[xpar][:, kc * 512:(kc + 1) * 512],
                                                 in_=src_ap),
                                     deps=[prev_xn[xpar]] if kc == 0 else [],
                                     tag=f"ldxn{ec}_{tb}_{kc}", slot=f"xnstg{xpar}"))
                slot = tb % 2
                mms = []
                for kc in range(8):
                    mms.append(p.op("tensor",
                                    lambda e, kc=kc, ec=ec, slot=slot, xpar=xpar:
                                    e.matmul(out=ps[slot][:], lhsT=inwb[:, kc * ESL + ec * 128:kc * ESL + (ec + 1) * 128],
                                             rhs=xnstg[xpar][:, kc * 512:(kc + 1) * 512],
                                             start=(kc == 0), stop=(kc == 7)),
                                    deps=(lds + [convs[-1], prev_ps[slot]]) if kc == 0 else [],
                                    tag=f"mm{ec}_{tb}_{kc}"))
                ev = p.op("vector",
                          lambda e, epar=epar, tb=tb, slot=slot:
                          e.tensor_copy(out=xin[epar][:, 3 + tb * 512:3 + (tb + 1) * 512], in_=ps[slot][:]),
                          deps=[mms[-1], z0], tag=f"ev{ec}_{tb}")
                prev_ps[slot] = ev
                prev_xn[xpar] = mms[-1]
                evs.append(ev)
            HF = S_ // 4
            silus = []
            for th in range(4):
                o = th * HF
                t1 = p.op("vector",
                          lambda e, epar=epar, ec=ec, o=o: e.tensor_scalar(
                              out=acc[0][:], in0=xin[epar][:, o:o + HF],
                              scalar1=cw[:, ec * 4:ec * 4 + 1], scalar2=cb[:, ec:ec + 1],
                              op0=ALU.mult, op1=ALU.add),
                          deps=evs + [ldcw, ldcb, prev_sil] + silus, tag=f"c1_{ec}_{th}")
                t2 = p.op("vector",
                          lambda e, epar=epar, ec=ec, o=o: e.scalar_tensor_tensor(
                              out=acc[1][:], in0=xin[epar][:, 1 + o:1 + o + HF],
                              scalar=cw[:, ec * 4 + 1:ec * 4 + 2], in1=acc[0][:],
                              op0=ALU.mult, op1=ALU.add),
                          deps=[t1], tag=f"c2_{ec}_{th}")
                t3 = p.op("vector",
                          lambda e, epar=epar, ec=ec, o=o: e.scalar_tensor_tensor(
                              out=acc[0][:], in0=xin[epar][:, 2 + o:2 + o + HF],
                              scalar=cw[:, ec * 4 + 2:ec * 4 + 3], in1=acc[1][:],
                              op0=ALU.mult, op1=ALU.add),
                          deps=[t2], tag=f"c3_{ec}_{th}")
                t4 = p.op("vector",
                          lambda e, epar=epar, ec=ec, o=o: e.scalar_tensor_tensor(
                              out=acc[1][:], in0=xin[epar][:, 3 + o:3 + o + HF],
                              scalar=cw[:, ec * 4 + 3:ec * 4 + 4], in1=acc[0][:],
                              op0=ALU.mult, op1=ALU.add),
                          deps=[t3], tag=f"c4_{ec}_{th}")
                sl_ = p.op("scalar",
                           lambda e, ec=ec, o=o: e.activation(out=xc[ec][:, o:o + HF], in_=acc[1][:],
                                                              func=AF.Silu),
                           deps=[t4], tag=f"sil{ec}_{th}")
                silus.append(sl_)
            prev_xinuse[epar] = silus[-1]
            prev_sil = silus[-1]
            conv_done[ec] = silus
            stx = p.dma("gpsimd", lambda e, ec=ec: e.dma_start(out=xc_d[ec * 128:(ec + 1) * 128, :], in_=xc[ec][:]),
                        deps=silus, tag=f"stxc{ec}", slot="f_st")
            xc_st.append(stx)
        prev_psb = [None, None]
        prev_bres = [None, None]
        stores = list(xc_st)
        for tb in range(NTB):
            slot = tb % 2
            mms = []
            for ec in range(4):
                mms.append(p.op("tensor",
                                lambda e, ec=ec, tb=tb, slot=slot:
                                e.matmul(out=ps_b[slot][:], lhsT=bcdb[:, ec * 96:(ec + 1) * 96],
                                         rhs=xc[ec][:, tb * 512:(tb + 1) * 512],
                                         start=(ec == 0), stop=(ec == 3)),
                                deps=(conv_done[0] + conv_done[1] + conv_done[2] + conv_done[3]
                                      + [bconvs[-1], prev_psb[slot]]) if ec == 0 else [],
                                tag=f"bm{tb}_{ec}"))
            ev = p.op("vector", lambda e, slot=slot: e.tensor_copy(out=bres[slot][:], in_=ps_b[slot][:]),
                      deps=[mms[-1], prev_bres[slot]], tag=f"bev{tb}")
            prev_psb[slot] = ev
            sto = p.dma("gpsimd", lambda e, tb=tb, slot=slot:
                        e.dma_start(out=bcdp_d[:, tb * 512:(tb + 1) * 512], in_=bres[slot][:]),
                        deps=[ev], tag=f"stb{tb}", slot=f"bres{slot}")
            prev_bres[slot] = sto
            stores.append(sto)
        p.barrier_op("gpsimd", stores, tag="final")
    finally:
        if es_own:
            es.close()


def build_F():
    nc = bass.Bass("TRN2", target_bir_lowering=False, debug=False)
    io = _io_F(nc)
    p = Prog(nc)
    p.slot_prefix = "F."
    emit_F(nc, p, io)
    with ExitStack() as es:
        block = es.enter_context(nc.Block())
        p.emit(es, block)
    return nc


def _io_G(nc):
    io = {}
    for j in range(4):
        io[f"p{j}"] = nc.dram_tensor(f"p{j}", [96, S_], F32, kind="ExternalInput").ap()
    io["xc"] = nc.dram_tensor("xc", [ESL, S_], BF16, kind="ExternalInput").ap()
    io["wd2T"] = nc.dram_tensor("wd2T", [DTR, ESL], F32, kind="ExternalInput").ap()
    io["wd2b"] = nc.dram_tensor("wd2b", [128, 4], F32, kind="ExternalInput").ap()
    io["alog"] = nc.dram_tensor("alog", [128, 4 * N_], F32, kind="ExternalInput").ap()
    io["wdsl"] = nc.dram_tensor("wdsl", [128, 4], F32, kind="ExternalInput").ap()
    io["ypp"] = nc.dram_tensor("ypp", [ESL, S_], BF16, kind="ExternalOutput").ap()
    io["bc_scr"] = nc.dram_tensor("bc_scr", [32, S_], BF16)
    return io


def emit_G(nc, p, io):
    pp_d = [io[f"p{j}"] for j in range(io.get("n_partials", 4))]
    xc_d = io["xc"]
    wd2T_d = io["wd2T"]
    wd2b_d = io["wd2b"]
    alog_d = io["alog"]
    wdsl_d = io["wdsl"]
    ypp_d = io["ypp"]
    bc_scr = io["bc_scr"]
    TB = 2048
    NTB = S_ // TB
    es_own = "es" not in io
    es = io.get("es") or ExitStack()
    n_part1 = io.get("n_partials", 4) == 1
    try:
        if n_part1:
            pst = []
            tsum = [None, None, es.enter_context(nc.sbuf_tensor("G." + "tsum2", [96, 512], F32))]
        else:
            pst = [es.enter_context(nc.sbuf_tensor("G." + f"pst{i}", [96, 512], F32)) for i in range(4)]
            tsum = [es.enter_context(nc.sbuf_tensor("G." + f"tsum{i}", [96, 512], F32)) for i in range(3)]
        bcbf = es.enter_context(nc.sbuf_tensor("G." + "bcbf", [96, S_], BF16))
        d1bf = es.enter_context(nc.sbuf_tensor("G." + "d1bf", [DTR, S_], BF16))
        w2b = es.enter_context(nc.sbuf_tensor("G." + "w2b", [DTR, ESL], BF16))
        w2stg = es.enter_context(nc.sbuf_tensor("G." + "w2stg", [DTR, ESL], F32))
        wd2b = es.enter_context(nc.sbuf_tensor("G." + "sb_wd2b", [128, 4], F32))
        alog = es.enter_context(nc.sbuf_tensor("G." + "sb_alog", [128, 4 * N_], F32))
        Asb = es.enter_context(nc.sbuf_tensor("G." + "Asb", [128, 4 * N_], F32))
        wdsl = es.enter_context(nc.sbuf_tensor("G." + "sb_wdsl", [128, 4], F32))
        dt = es.enter_context(nc.sbuf_tensor("G." + "dt", [128, S_], F32))
        sc0 = es.enter_context(nc.sbuf_tensor("G." + "sc0", [128, S_ // 8], F32))
        sc1 = es.enter_context(nc.sbuf_tensor("G." + "sc1", [128, S_ // 8], F32))
        xcs = [es.enter_context(nc.sbuf_tensor("G." + "xcs0", [128, S_], BF16))]
        dtxb = es.enter_context(nc.sbuf_tensor("G." + "dtxb", [128, S_], BF16))
        Bb = [es.enter_context(nc.sbuf_tensor("G." + f"Bb{i}", [128, TB], BF16)) for i in range(2)]
        Cb = [es.enter_context(nc.sbuf_tensor("G." + f"Cb{i}", [128, TB], BF16)) for i in range(2)]
        ab = [es.enter_context(nc.sbuf_tensor("G." + f"ab{i}", [128, TB], F32)) for i in range(2)]
        ub = es.enter_context(nc.sbuf_tensor("G." + "ub", [128, TB], BF16))
        hb = es.enter_context(nc.sbuf_tensor("G." + "hb", [128, TB], BF16))
        hct = es.enter_context(nc.sbuf_tensor("G." + "hct", [128, TB], BF16))
        yp = [es.enter_context(nc.sbuf_tensor("G." + f"yp{i}", [128, TB], BF16)) for i in range(2)]
        yout = [es.enter_context(nc.sbuf_tensor("G." + f"yout{i}", [128, TB], BF16)) for i in range(2)]
        carry = es.enter_context(nc.sbuf_tensor("G." + "carry", [128, N_], F32))
        ps_d = [es.enter_context(nc.psum_tensor("G." + f"psd{i}", [128, 512], F32)) for i in range(2)]
        CV = CVals(nc, es, p, [])

        n_part = io.get("n_partials", 4)
        prev_p = [None] * 4
        prev_ts = [None, None]
        prev_bs = None
        bcd_parts = []
        for tb5 in range(S_ // 512):
            if n_part == 1:
                ccdep = io.get("ccdep")
                ld0 = p.dma("gpsimd" if ccdep is not None else "sync",
                            lambda e, tb5=tb5: e.dma_start(out=tsum[2][:], in_=pp_d[0][:, tb5 * 512:(tb5 + 1) * 512]),
                            deps=[prev_bs, ccdep], tag=f"ldp{tb5}", slot="pst0")
                bsum = ld0
            else:
                lds = []
                for j in range(4):
                    lds.append(p.dma("sync",
                                     lambda e, j=j, tb5=tb5: e.dma_start(out=pst[j][:], in_=pp_d[j][:, tb5 * 512:(tb5 + 1) * 512]),
                                     deps=[prev_p[j]], tag=f"ldp{tb5}_{j}", slot=f"pst{j}"))
                t01 = p.op("vector", lambda e: e.tensor_add(out=tsum[0][:], in0=pst[0][:], in1=pst[1][:]),
                           deps=lds[0:2] + [prev_ts[0]], tag=f"t01_{tb5}")
                t23 = p.op("vector", lambda e: e.tensor_add(out=tsum[1][:], in0=pst[2][:], in1=pst[3][:]),
                           deps=lds[2:4] + [prev_ts[1]], tag=f"t23_{tb5}")
                prev_p[0] = prev_p[1] = t01
                prev_p[2] = prev_p[3] = t23
                bsum = p.op("vector", lambda e: e.tensor_add(out=tsum[2][:], in0=tsum[0][:], in1=tsum[1][:]),
                            deps=[t01, t23, prev_bs], tag=f"bsum{tb5}")
                prev_ts[0] = prev_ts[1] = bsum
            cv2 = p.op("scalar", lambda e, tb5=tb5: e.activation(out=d1bf[:, tb5 * 512:(tb5 + 1) * 512],
                                                                 in_=tsum[2][0:64, :], func=AF.Copy),
                       deps=[bsum], tag=f"cvd1{tb5}")
            cv1 = p.op("scalar", lambda e, tb5=tb5: e.activation(out=bcbf[64:96, tb5 * 512:(tb5 + 1) * 512],
                                                                 in_=tsum[2][64:96, :], func=AF.Copy),
                       deps=[bsum], tag=f"cvbc{tb5}")
            stt_ = p.dma("gpsimd", lambda e, tb5=tb5: e.dma_start(out=bc_scr.ap()[:, tb5 * 512:(tb5 + 1) * 512],
                                                                  in_=bcbf[64:96, tb5 * 512:(tb5 + 1) * 512]),
                         deps=[cv1], tag=f"stbc{tb5}", slot="f_bc")
            prev_bs = cv1
            bcd_parts += [cv2, stt_]
        stbc = p.barrier_op("gpsimd", bcd_parts, tag="bcdone")
        cvd1 = p.barrier_op("scalar", bcd_parts, tag="d1done")
        ldw2 = p.dma("sync", lambda e: e.dma_start(out=w2stg[:], in_=wd2T_d), tag="ldw2", slot="f_init")
        cvw2 = p.op("scalar", lambda e: e.activation(out=w2b[:], in_=w2stg[:], func=AF.Copy),
                    deps=[ldw2], tag="cvw2")
        ldb2 = p.dma("sync", lambda e: e.dma_start(out=wd2b[:], in_=wd2b_d), tag="ldb2", slot="f_init")
        lda = p.dma("sync", lambda e: e.dma_start(out=alog[:], in_=alog_d), tag="lda", slot="f_init")
        ldwd = p.dma("sync", lambda e: e.dma_start(out=wdsl[:], in_=wdsl_d), tag="ldwd", slot="f_init")
        aexp = p.op("scalar", lambda e: e.activation(out=Asb[:], in_=alog[:], func=AF.Exp),
                    deps=[lda], tag="aexp")
        aneg = p.op("vector", lambda e: e.tensor_scalar(out=Asb[:], in0=Asb[:], scalar1=-1.0,
                                                        scalar2=None, op0=ALU.mult),
                    deps=[aexp], tag="aneg")

        stores = []
        prev_xcs = [None]
        last_cupd = None
        prev_scan1 = None
        prev_scan2 = None
        prev_dt_user = None
        prev_bb = [None, None]
        prev_cbb = [None, None]
        prev_yout = [None, None]
        for ec in range(4):
            xpar = 0
            ldxc = p.dma("sync", lambda e, ec=ec: e.dma_start(out=xcs[0][:], in_=xc_d[ec * 128:(ec + 1) * 128, :]),
                         deps=[prev_xcs[0]], tag=f"ldxc{ec}", slot="xcs0")
            devs = []
            for tb5 in range(S_ // 512):
                slot = tb5 % 2
                mm = p.op("tensor",
                          lambda e, ec=ec, tb5=tb5, slot=slot:
                          e.matmul(out=ps_d[slot][:], lhsT=w2b[:, ec * 128:(ec + 1) * 128],
                                   rhs=d1bf[:, tb5 * 512:(tb5 + 1) * 512], start=True, stop=True),
                          deps=[cvw2, cvd1]
                          + ([prev_dt_user] if tb5 == 0 else [])
                          + ([devs[-2]] if len(devs) >= 2 else []),
                          tag=f"dmm{ec}_{tb5}")
                ev = p.op("vector",
                          lambda e, ec=ec, tb5=tb5, slot=slot:
                          e.tensor_scalar(out=dt[:, tb5 * 512:(tb5 + 1) * 512], in0=ps_d[slot][:],
                                          scalar1=wd2b[:, ec:ec + 1], scalar2=None, op0=ALU.add),
                          deps=[mm, ldb2], tag=f"dev{ec}_{tb5}")
                devs.append(ev)
            sps = []
            for th in range(8):
                o = th * (S_ // 8)
                HF = S_ // 8
                ab_ = p.op("scalar", lambda e, o=o, HF=HF: e.activation(out=sc0[:], in_=dt[:, o:o + HF], func=AF.Abs),
                           deps=devs + sps, tag=f"abs{ec}_{th}")
                ex_ = p.op("scalar", lambda e: e.activation(out=sc1[:], in_=sc0[:], func=AF.Exp, scale=-1.0),
                           deps=[ab_], tag=f"exp{ec}_{th}")
                ln_ = p.op("scalar", lambda e: e.activation(out=sc0[:], in_=sc1[:], func=AF.Ln, bias=1.0),
                           deps=[ex_], tag=f"ln{ec}_{th}")
                rl_ = p.op("scalar", lambda e, o=o, HF=HF: e.activation(out=sc1[:], in_=dt[:, o:o + HF], func=AF.Relu),
                           deps=[ln_], tag=f"rl{ec}_{th}")
                sp_ = p.op("vector", lambda e, o=o, HF=HF: e.tensor_add(out=dt[:, o:o + HF], in0=sc0[:], in1=sc1[:]),
                           deps=[rl_], tag=f"sp{ec}_{th}")
                sps.append(sp_)
            dtx = p.op("vector", lambda e, xpar=xpar: e.tensor_mul(out=dtxb[:], in0=dt[:], in1=xcs[xpar][:]),
                       deps=sps + [ldxc, prev_dt_user], tag=f"dtx{ec}")
            zc = p.op("vector", lambda e: e.memset(carry[:], 0.0),
                      deps=[prev_dt_user, last_cupd], tag=f"zc{ec}")
            ylast = None
            for tb in range(NTB):
                o = tb * TB
                for n in range(N_):
                    i2 = (tb * N_ + n) % 2
                    ldB = p.dma("sync",
                                lambda e, n=n, o=o, i2=i2: e.dma_start(
                                    out=Bb[i2][:],
                                    in_=bass.AP(bc_scr, n * S_ + o, [[0, 128], [1, TB]])),
                                deps=[stbc, prev_bb[i2]], tag=f"ldB{ec}_{tb}_{n}", slot=f"Bb{i2}")
                    ldC = p.dma("sync",
                                lambda e, n=n, o=o, i2=i2: e.dma_start(
                                    out=Cb[i2][:],
                                    in_=bass.AP(bc_scr, (16 + n) * S_ + o, [[0, 128], [1, TB]])),
                                deps=[stbc, prev_cbb[i2]], tag=f"ldC{ec}_{tb}_{n}", slot=f"Cb{i2}")
                    asl = (tb * N_ + n) % 2
                    abar = p.op("scalar",
                                lambda e, ec=ec, n=n, o=o, asl=asl:
                                e.activation(out=ab[asl][:], in_=dt[:, o:o + TB],
                                             func=AF.Exp,
                                             scale=Asb[:, ec * N_ + n:ec * N_ + n + 1]),
                                deps=[sps[-1], aneg] + ([prev_scan2] if (tb * N_ + n) >= 2 else []),
                                tag=f"abar{ec}_{tb}_{n}")
                    uo = p.op("vector",
                              lambda e, i2=i2, o=o: e.tensor_mul(out=ub[:], in0=dtxb[:, o:o + TB], in1=Bb[i2][:]),
                              deps=[dtx, ldB] + ([scano] if (tb + n) else []), tag=f"u{ec}_{tb}_{n}")
                    prev_bb[i2] = uo
                    scano = p.op("vector",
                                 lambda e, n=n, asl=asl: e.tensor_tensor_scan(out=hb[:], data0=ab[asl][:],
                                                                              data1=ub[:],
                                                                              initial=carry[:, n:n + 1],
                                                                              op0=ALU.mult, op1=ALU.add),
                                 deps=[abar, uo, zc] + ([cupd, hco] if (tb + n) else []),
                                 tag=f"scan{ec}_{tb}_{n}")
                    prev_scan2 = prev_scan1
                    prev_scan1 = scano
                    cupd = p.op("vector",
                                lambda e, n=n: e.tensor_copy(out=carry[:, n:n + 1], in_=hb[:, TB - 1:TB]),
                                deps=[scano], tag=f"cu{ec}_{tb}_{n}")
                    last_cupd = cupd
                    hco = p.op("vector",
                               lambda e, i2=i2: e.tensor_mul(out=hct[:], in0=hb[:], in1=Cb[i2][:]),
                               deps=[scano, ldC] + ([yac] if (tb + n) else []), tag=f"hc{ec}_{tb}_{n}")
                    prev_cbb[i2] = hco
                    if n == 0:
                        yac = p.op("vector", lambda e: e.tensor_copy(out=yp[0][:], in_=hct[:]),
                                   deps=[hco] + ([ylast] if ylast is not None else []), tag=f"y0{ec}_{tb}")
                    else:
                        yac = p.op("vector",
                                   lambda e, n=n: e.tensor_add(out=yp[n % 2][:], in0=yp[(n + 1) % 2][:], in1=hct[:]),
                                   deps=[hco, yac], tag=f"ya{ec}_{tb}_{n}")
                ysl = tb % 2
                yo = p.op("vector",
                          lambda e, xpar=xpar, ec=ec, o=o, ysl=ysl:
                          e.scalar_tensor_tensor(out=yout[ysl][:], in0=xcs[xpar][:, o:o + TB],
                                                 scalar=wdsl[:, ec:ec + 1], in1=yp[(N_ - 1) % 2][:],
                                                 op0=ALU.mult, op1=ALU.add),
                          deps=[yac, ldwd, prev_yout[ysl]], tag=f"yo{ec}_{tb}")
                ylast = yo
                sto = p.dma("gpsimd",
                            lambda e, ec=ec, o=o, ysl=ysl:
                            e.dma_start(out=ypp_d[ec * 128:(ec + 1) * 128, o:o + TB], in_=yout[ysl][:]),
                            deps=[yo], tag=f"sty{ec}_{tb}", slot=f"yout{ysl}")
                prev_yout[ysl] = sto
                stores.append(sto)
            prev_xcs[0] = yo
            prev_dt_user = yo
        p.barrier_op("gpsimd", stores, tag="final")
    finally:
        if es_own:
            es.close()


def build_G():
    nc = bass.Bass("TRN2", target_bir_lowering=False, debug=False)
    io = _io_G(nc)
    p = Prog(nc)
    p.set_stage("G.")
    emit_G(nc, p, io)
    with ExitStack() as es:
        block = es.enter_context(nc.Block())
        p.emit(es, block)
    return nc


def emit_H1w(nc, p, io):
    skT_d = io["skT"]
    owT_d = io["owT"]
    skb_d = io["skb"]
    owb_d = io["owb"]
    es = io["es"]
    wstg = [es.enter_context(nc.sbuf_tensor("Hw." + f"wstg{i}", [128, 1024], F32)) for i in range(2)]
    wbf = [es.enter_context(nc.sbuf_tensor("Hw." + f"wbf{i}", [128, 1024], BF16)) for i in range(2)]
    tiles = []
    for kc in range(8):
        for h in range(2):
            tiles.append((skT_d[kc * 128:(kc + 1) * 128, h * 1024:(h + 1) * 1024],
                          skb_d[kc * 128:(kc + 1) * 128, h * 1024:(h + 1) * 1024]))
    for kc in range(16):
        tiles.append((owT_d[kc * 128:(kc + 1) * 128, :], owb_d[kc * 128:(kc + 1) * 128, :]))
    prev_cv = [None, None]
    prev_st = [None, None]
    for i, (src, dst) in enumerate(tiles):
        par = i % 2
        ld = p.dma("gpsimd", lambda e, src=src, par=par: e.dma_start(out=wstg[par][:], in_=src),
                   deps=[prev_cv[par]], tag=f"hwld{i}", slot=f"hw{par}")
        cv = p.op("scalar", lambda e, par=par: e.activation(out=wbf[par][:], in_=wstg[par][:], func=AF.Copy),
                  deps=[ld, prev_st[par]], tag=f"hwcv{i}")
        prev_cv[par] = cv
        st = p.dma("gpsimd", lambda e, dst=dst, par=par: e.dma_start(out=dst, in_=wbf[par][:]),
                   deps=[cv], tag=f"hwst{i}", slot=f"hws{par}")
        prev_st[par] = st


def emit_H1a(nc, p, io):
    p.rank_engines.add("gpsimd")
    xn2_d = io["xn2f"]
    skbd = io["skbd"]
    owbd = io["owbd"]
    yoff_d = io["yoff"]
    es = io["es"]
    skb = es.enter_context(nc.sbuf_tensor("H1." + "skb", [128, 8 * E_], BF16))
    owb = es.enter_context(nc.sbuf_tensor("H1." + "owb", [128, 16 * D_], BF16))
    xnf = [es.enter_context(nc.sbuf_tensor("H1." + f"xnf{i}", [128, TC], BF16)) for i in range(8)]
    ysg = [es.enter_context(nc.sbuf_tensor("H1." + f"ysg{i}", [128, TC], BF16)) for i in range(16)]
    ps = [es.enter_context(nc.psum_tensor("H1." + f"ps{i}", [128, 512], F32)) for i in range(2)]
    yoffs = es.enter_context(nc.sbuf_tensor("H1.yoffs", [1, 1], I32))
    ldyoff = p.dma("sync", lambda e: e.dma_start(out=yoffs[:], in_=yoff_d),
                   tag="ldyoff", slot="f_init")
    ldxn = [p.dma("sync", lambda e, dc=dc: e.dma_start(out=xnf[dc][:], in_=xn2_d[dc * 128:(dc + 1) * 128, :]),
                  tag=f"ldxn{dc}", slot="f_init") for dc in range(8)]
    for kc in range(8):
        skld = p.dma("sync", lambda e, kc=kc: e.dma_start(out=skb[:, kc * E_:(kc + 1) * E_],
                                                          in_=skbd[kc * 128:(kc + 1) * 128, :]),
                     tag=f"ldskb{kc}", slot="f_init")
    for kc in range(16):
        owld = p.dma("sync", lambda e, kc=kc: e.dma_start(out=owb[:, kc * D_:(kc + 1) * D_],
                                                          in_=owbd[kc * 128:(kc + 1) * 128, :]),
                     tag=f"ldowb{kc}", slot="f_init")
    sils = {}
    prev_ps = [None, None]
    for ec in range(16):
        for tb in range(4):
            slot = (ec * 4 + tb) % 2
            mms = []
            for kc in range(8):
                mms.append(p.op("tensor",
                                lambda e, kc=kc, ec=ec, tb=tb, slot=slot:
                                e.matmul(out=ps[slot][:], lhsT=skb[:, kc * E_ + ec * 128:kc * E_ + (ec + 1) * 128],
                                         rhs=xnf[kc][:, tb * 512:(tb + 1) * 512],
                                         start=(kc == 0), stop=(kc == 7)),
                                deps=(ldxn + [skld, prev_ps[slot]]) if kc == 0 else [],
                                tag=f"skm{ec}_{tb}_{kc}"))
            sil = p.op("scalar",
                       lambda e, ec=ec, tb=tb, slot=slot:
                       e.activation(out=ysg[ec][:, tb * 512:(tb + 1) * 512], in_=ps[slot][:], func=AF.Silu),
                       deps=[mms[-1], prev_ps[slot]], tag=f"sil{ec}_{tb}")
            prev_ps[slot] = sil
            sils[(ec, tb)] = sil
    return {"skb": skb, "owb": owb, "xnf": xnf, "ysg": ysg, "yoffs": yoffs,
            "ldyoff": ldyoff, "owld": owld, "sils": sils}


def emit_H1b(nc, p, io, st):
    ypp_t = io["ypp_ag_tensor"]
    xn1_d = io["xn1tm"]
    s1_d = io["s1"]
    es = io["es"]
    ysg = st["ysg"]
    owb = st["owb"]
    yoffs = st["yoffs"]
    NTC = TC // 128
    yt = [es.enter_context(nc.sbuf_tensor("H1." + f"yt{i}", [128, TC], BF16)) for i in range(2)]
    xn1t = [es.enter_context(nc.sbuf_tensor("H1." + f"xn1t{i}", [128, D_], BF16)) for i in range(2)]
    s1t = [es.enter_context(nc.sbuf_tensor("H1." + f"s1t{i}", [128, 512], BF16)) for i in range(2)]
    ps_o = [es.enter_context(nc.psum_tensor("H1." + f"pso{i}", [128, 512], F32)) for i in range(2)]
    ysg_done = []
    prev_yt = [None, None]
    for ec in range(16):
        ypar = ec % 2

        def _ldy_fn(e, ec=ec, ypar=ypar, first=(ec == 0)):
            rk, tmp = p.regs["gpsimd"]
            if first:
                e.reg_load(rk, yoffs[:1, :1])
            e.reg_mov(tmp, ec * 128 * S_)
            e.reg_add(tmp, tmp, rk)
            return e.dma_start(out=yt[ypar][:],
                               in_=bass.AP(ypp_t, tmp, [[S_, 128], [1, TC]]))
        ldy = p.dma("gpsimd", _ldy_fn, deps=[prev_yt[ypar], st["ldyoff"], io.get("cc")],
                    tag=f"ldy{ec}", slot=f"yt{ypar}")
        last = None
        for tb in range(4):
            yg = p.op("vector",
                      lambda e, ec=ec, tb=tb, ypar=ypar:
                      e.tensor_mul(out=ysg[ec][:, tb * 512:(tb + 1) * 512],
                                   in0=yt[ypar][:, tb * 512:(tb + 1) * 512],
                                   in1=ysg[ec][:, tb * 512:(tb + 1) * 512]),
                      deps=[st["sils"][(ec, tb)], ldy], tag=f"yg{ec}_{tb}")
            last = yg
            ysg_done.append(yg)
        prev_yt[ypar] = last
    stores = []
    prev_x1 = [None, None]
    prev_pso = [None, None]
    prev_s1t = [None, None]
    for tc in range(NTC):
        xpar = tc % 2
        ldx1 = p.dma("sync", lambda e, tc=tc, xpar=xpar: e.dma_start(out=xn1t[xpar][:], in_=xn1_d[tc * 128:(tc + 1) * 128, :]),
                     deps=[prev_x1[xpar]], tag=f"ldx1{tc}", slot=f"xn1t{xpar}")
        for ob in range(2):
            slot = ob
            mms = []
            for kc in range(16):
                mms.append(p.op("tensor",
                                lambda e, kc=kc, tc=tc, ob=ob, slot=slot:
                                e.matmul(out=ps_o[slot][:], lhsT=ysg[kc][:, tc * 128:(tc + 1) * 128],
                                         rhs=owb[:, kc * D_ + ob * 512:kc * D_ + ob * 512 + 512],
                                         start=(kc == 0), stop=(kc == 15)),
                                deps=(ysg_done + [st["owld"], prev_pso[slot]]) if kc == 0 else [],
                                tag=f"om{tc}_{ob}_{kc}"))
            ev = p.op("vector",
                      lambda e, tc=tc, ob=ob, slot=slot, xpar=xpar:
                      e.tensor_add(out=s1t[slot][:], in0=ps_o[slot][:],
                                   in1=xn1t[xpar][:, ob * 512:ob * 512 + 512]),
                      deps=[mms[-1], ldx1, prev_s1t[slot]], tag=f"oev{tc}_{ob}")
            prev_pso[slot] = ev
            sto = p.dma("gpsimd",
                        lambda e, tc=tc, ob=ob, slot=slot:
                        e.dma_start(out=s1_d[tc * 128:(tc + 1) * 128, ob * 512:ob * 512 + 512],
                                    in_=s1t[slot][:]),
                        deps=[ev], tag=f"sts1{tc}_{ob}", slot=f"s1t{slot}")
            prev_s1t[slot] = sto
            stores.append(sto)
        prev_x1[xpar] = ev
    p.barrier_op("gpsimd", stores, tag="final")


def _io_H1(nc):
    io = {}
    io["yppt"] = nc.dram_tensor("yppt", [E_, TC], BF16, kind="ExternalInput").ap()
    io["xn2f"] = nc.dram_tensor("xn2f", [D_, TC], BF16, kind="ExternalInput").ap()
    io["xn1tm"] = nc.dram_tensor("xn1tm", [TC, D_], BF16, kind="ExternalInput").ap()
    io["skT"] = nc.dram_tensor("skT", [D_, E_], F32, kind="ExternalInput").ap()
    io["owT"] = nc.dram_tensor("owT", [E_, D_], F32, kind="ExternalInput").ap()
    io["s1"] = nc.dram_tensor("s1", [TC, D_], BF16, kind="ExternalOutput").ap()
    return io


def emit_H1(nc, p, io):
    ag_mode = "ypp_is_ag" in io
    if ag_mode:
        p.rank_engines.add("gpsimd")
        ypp_t = io["ypp_ag_tensor"]
        yoff_d = io["yoff"]
    ypp_d = io.get("yppt")
    xn2_d = io["xn2f"]
    xn1_d = io["xn1tm"]
    pre = "skbd" in io
    skT_d = io.get("skT")
    owT_d = io.get("owT")
    s1_d = io["s1"]
    NTC = TC // 128
    es_own = "es" not in io
    es = io.get("es") or ExitStack()
    try:
        skb = es.enter_context(nc.sbuf_tensor("H1." + "skb", [128, 8 * E_], BF16))
        owb = es.enter_context(nc.sbuf_tensor("H1." + "owb", [128, 16 * D_], BF16))
        if not pre:
            wstg = [es.enter_context(nc.sbuf_tensor("H1." + f"wstg{i}", [128, E_], F32)) for i in range(2)]
        xnf = [es.enter_context(nc.sbuf_tensor("H1." + f"xnf{i}", [128, TC], BF16)) for i in range(8)]
        yt = [es.enter_context(nc.sbuf_tensor("H1." + f"yt{i}", [128, TC], BF16)) for i in range(2)]
        sst = [es.enter_context(nc.sbuf_tensor("H1." + f"sst{i}", [128, 512], BF16)) for i in range(2)]
        ysg = [es.enter_context(nc.sbuf_tensor("H1." + f"ysg{i}", [128, TC], BF16)) for i in range(16)]
        xn1t = [es.enter_context(nc.sbuf_tensor("H1." + f"xn1t{i}", [128, D_], BF16)) for i in range(2)]
        s1t = [es.enter_context(nc.sbuf_tensor("H1." + f"s1t{i}", [128, 512], BF16)) for i in range(2)]
        ps = [es.enter_context(nc.psum_tensor("H1." + f"ps{i}", [128, 512], F32)) for i in range(2)]
        ps_o = [es.enter_context(nc.psum_tensor("H1." + f"pso{i}", [128, 512], F32)) for i in range(2)]

        ldyoff = None
        if ag_mode:
            yoffs = es.enter_context(nc.sbuf_tensor("H1.yoffs", [1, 1], I32))
            ldyoff = p.dma("sync", lambda e: e.dma_start(out=yoffs[:], in_=yoff_d),
                           tag="ldyoff", slot="f_init")
        ldxn = [p.dma("sync", lambda e, dc=dc: e.dma_start(out=xnf[dc][:], in_=xn2_d[dc * 128:(dc + 1) * 128, :]),
                      tag=f"ldxn{dc}", slot="f_init") for dc in range(8)]
        if pre:
            skbd = io["skbd"]
            owbd = io["owbd"]
            for kc in range(8):
                skcv = p.dma("sync", lambda e, kc=kc: e.dma_start(out=skb[:, kc * E_:(kc + 1) * E_],
                                                                  in_=skbd[kc * 128:(kc + 1) * 128, :]),
                             tag=f"ldskb{kc}", slot="f_init")
            for kc in range(16):
                owcv = p.dma("sync", lambda e, kc=kc: e.dma_start(out=owb[:, kc * D_:(kc + 1) * D_],
                                                                  in_=owbd[kc * 128:(kc + 1) * 128, :]),
                             tag=f"ldowb{kc}", slot="f_init")
        else:
            prev_w = [None, None]
            wcv = []
            for kc in range(8):
                ldw = p.dma("sync", lambda e, kc=kc: e.dma_start(out=wstg[kc % 2][:], in_=skT_d[kc * 128:(kc + 1) * 128, :]),
                            deps=[prev_w[kc % 2]], tag=f"ldsk{kc}", slot=f"wstg{kc % 2}")
                cv_ = p.op("scalar", lambda e, kc=kc: e.activation(out=skb[:, kc * E_:(kc + 1) * E_],
                                                                   in_=wstg[kc % 2][:], func=AF.Copy),
                           deps=[ldw], tag=f"cvsk{kc}")
                prev_w[kc % 2] = cv_
                wcv.append(cv_)
            for kc in range(16):
                j = 8 + kc
                ldw = p.dma("sync", lambda e, kc=kc: e.dma_start(out=wstg[j % 2][:, 0:D_], in_=owT_d[kc * 128:(kc + 1) * 128, :]),
                            deps=[prev_w[j % 2]], tag=f"ldow{kc}", slot=f"wstg{j % 2}")
                cv_ = p.op("scalar", lambda e, kc=kc, j=j: e.activation(out=owb[:, kc * D_:(kc + 1) * D_],
                                                                        in_=wstg[j % 2][:, 0:D_], func=AF.Copy),
                           deps=[ldw], tag=f"cvow{kc}")
                prev_w[j % 2] = cv_
                wcv.append(cv_)
            skcv = wcv[7]
            owcv = wcv[-1]

        ysg_done = []
        prev_yt = [None, None]
        prev_ps = [None, None]
        prev_sst = [None, None]
        for ec in range(16):
            ypar = ec % 2
            if ag_mode:
                def _ldy_fn(e, ec=ec, ypar=ypar, first=(ec == 0)):
                    rk, tmp = p.regs["gpsimd"]
                    if first:
                        e.reg_load(rk, yoffs[:1, :1])
                    e.reg_mov(tmp, ec * 128 * S_)
                    e.reg_add(tmp, tmp, rk)
                    return e.dma_start(out=yt[ypar][:],
                                       in_=bass.AP(ypp_t, tmp, [[S_, 128], [1, TC]]))
                ldy = p.dma("gpsimd", _ldy_fn, deps=[prev_yt[ypar], ldyoff],
                            tag=f"ldy{ec}", slot=f"yt{ypar}")
            else:
                ldy = p.dma("sync", lambda e, ec=ec, ypar=ypar: e.dma_start(out=yt[ypar][:], in_=ypp_d[ec * 128:(ec + 1) * 128, :]),
                            deps=[prev_yt[ypar]], tag=f"ldy{ec}", slot=f"yt{ypar}")
            last = None
            for tb in range(4):
                slot = tb % 2
                mms = []
                for kc in range(8):
                    mms.append(p.op("tensor",
                                    lambda e, kc=kc, ec=ec, tb=tb, slot=slot:
                                    e.matmul(out=ps[slot][:], lhsT=skb[:, kc * E_ + ec * 128:kc * E_ + (ec + 1) * 128],
                                             rhs=xnf[kc][:, tb * 512:(tb + 1) * 512],
                                             start=(kc == 0), stop=(kc == 7)),
                                    deps=(ldxn + [skcv, prev_ps[slot]]) if kc == 0 else [],
                                    tag=f"skm{ec}_{tb}_{kc}"))
                sil = p.op("scalar", lambda e, slot=slot: e.activation(out=sst[slot][:], in_=ps[slot][:], func=AF.Silu),
                           deps=[mms[-1], prev_sst[slot]], tag=f"sil{ec}_{tb}")
                prev_ps[slot] = sil
                yg = p.op("vector",
                          lambda e, ec=ec, tb=tb, slot=slot, ypar=ypar:
                          e.tensor_mul(out=ysg[ec][:, tb * 512:(tb + 1) * 512],
                                       in0=yt[ypar][:, tb * 512:(tb + 1) * 512], in1=sst[slot][:]),
                          deps=[sil, ldy], tag=f"yg{ec}_{tb}")
                prev_sst[slot] = yg
                last = yg
                ysg_done.append(yg)
            prev_yt[ypar] = last
        stores = []
        prev_x1 = [None, None]
        prev_pso = [None, None]
        prev_s1t = [None, None]
        for tc in range(NTC):
            xpar = tc % 2
            ldx1 = p.dma("sync", lambda e, tc=tc, xpar=xpar: e.dma_start(out=xn1t[xpar][:], in_=xn1_d[tc * 128:(tc + 1) * 128, :]),
                         deps=[prev_x1[xpar]], tag=f"ldx1{tc}", slot=f"xn1t{xpar}")
            for ob in range(2):
                slot = ob
                mms = []
                for kc in range(16):
                    mms.append(p.op("tensor",
                                    lambda e, kc=kc, tc=tc, ob=ob, slot=slot:
                                    e.matmul(out=ps_o[slot][:], lhsT=ysg[kc][:, tc * 128:(tc + 1) * 128],
                                             rhs=owb[:, kc * D_ + ob * 512:kc * D_ + ob * 512 + 512],
                                             start=(kc == 0), stop=(kc == 15)),
                                    deps=(ysg_done + [owcv, prev_pso[slot]]) if kc == 0 else [],
                                    tag=f"om{tc}_{ob}_{kc}"))
                ev = p.op("vector",
                          lambda e, tc=tc, ob=ob, slot=slot, xpar=xpar:
                          e.tensor_add(out=s1t[slot][:], in0=ps_o[slot][:],
                                       in1=xn1t[xpar][:, ob * 512:ob * 512 + 512]),
                          deps=[mms[-1], ldx1, prev_s1t[slot]], tag=f"oev{tc}_{ob}")
                prev_pso[slot] = ev
                sto = p.dma("gpsimd",
                            lambda e, tc=tc, ob=ob, slot=slot:
                            e.dma_start(out=s1_d[tc * 128:(tc + 1) * 128, ob * 512:ob * 512 + 512],
                                        in_=s1t[slot][:]),
                            deps=[ev], tag=f"sts1{tc}_{ob}", slot=f"s1t{slot}")
                prev_s1t[slot] = sto
                stores.append(sto)
            prev_x1[xpar] = ev
        p.barrier_op("gpsimd", stores, tag="final")
    finally:
        if es_own:
            es.close()


def build_H1():
    nc = bass.Bass("TRN2", target_bir_lowering=False, debug=False)
    io = _io_H1(nc)
    p = Prog(nc)
    p.set_stage("H1.")
    emit_H1(nc, p, io)
    with ExitStack() as es:
        block = es.enter_context(nc.Block())
        p.emit(es, block)
    return nc


def _io_H2(nc):
    io = {}
    io["xn1f"] = nc.dram_tensor("xn1f", [D_, TC], BF16, kind="ExternalInput").ap()
    io["w1"] = nc.dram_tensor("w1", [HID1, D_], F32, kind="ExternalInput").ap()
    io["b1"] = nc.dram_tensor("b1", [128, HID1 // 128], F32, kind="ExternalInput").ap()
    io["ident"] = nc.dram_tensor("ident", [128, 128], BF16, kind="ExternalInput").ap()
    io["h1"] = nc.dram_tensor("h1", [HID1, TC], BF16, kind="ExternalOutput").ap()
    return io


def emit_H2(nc, p, io):
    xn1f_d = io["xn1f"]
    w1_d = io["w1"]
    b1_d = io["b1"]
    id_d = io["ident"]
    h1_d = io["h1"]
    NH = HID1 // 128
    es_own = "es" not in io
    es = io.get("es") or ExitStack()
    try:
        xnf = [es.enter_context(nc.sbuf_tensor("H2." + f"xnf{i}", [128, TC], BF16)) for i in range(8)]
        wstg = [es.enter_context(nc.sbuf_tensor("H2." + f"wstg{i}", [128, D_], F32)) for i in range(2)]
        sgn = es.enter_context(nc.sbuf_tensor("H2." + "sgn", [128, D_], BF16))
        bwrow = [es.enter_context(nc.sbuf_tensor("H2." + f"bwrow{i}", [128, D_], BF16)) for i in range(2)]
        alpha = es.enter_context(nc.sbuf_tensor("H2." + "alpha", [128, 4], F32))
        bwcol = [es.enter_context(nc.sbuf_tensor("H2." + f"bwcol{i}", [128, 8 * 128], BF16)) for i in range(2)]
        b1 = es.enter_context(nc.sbuf_tensor("H2." + "sb_b1", [128, NH], F32))
        ident = es.enter_context(nc.sbuf_tensor("H2." + "s_ident", [128, 128], BF16))
        hres = [es.enter_context(nc.sbuf_tensor("H2." + f"hres{i}", [128, 512], BF16)) for i in range(2)]
        ps_tr = [es.enter_context(nc.psum_tensor("H2." + f"pstr{i}", [128, 128], BF16)) for i in range(2)]
        ps = [es.enter_context(nc.psum_tensor("H2." + f"ps{i}", [128, 512], F32)) for i in range(2)]

        ldxn = [p.dma("sync", lambda e, dc=dc: e.dma_start(out=xnf[dc][:], in_=xn1f_d[dc * 128:(dc + 1) * 128, :]),
                      tag=f"ldxn{dc}", slot="f_init") for dc in range(8)]
        ldid = p.dma("sync", lambda e: e.dma_start(out=ident[:], in_=id_d), tag="ldid", slot="f_init")
        ldb1 = p.dma("sync", lambda e: e.dma_start(out=b1[:], in_=b1_d), tag="ldb1", slot="f_init")

        prev_w = [None, None]
        prev_bw = [None, None]
        prev_tr = [None, None]
        prev_bwcol = [None, None]
        last_bw = None
        stores = []
        prev_ps = [None, None]
        prev_hres = [None, None]
        for hc in range(NH):
            par = hc % 2
            ldw = p.dma("sync", lambda e, hc=hc, par=par: e.dma_start(out=wstg[par][:], in_=w1_d[hc * 128:(hc + 1) * 128, :]),
                        deps=[prev_w[par]], tag=f"ldw{hc}", slot=f"wstg{par}")
            red = p.op("vector", lambda e, par=par: e.tensor_reduce(out=alpha[:, 0:1], in_=wstg[par][:],
                                                                    axis=AX.X, op=ALU.add,
                                                                    apply_absolute_value=True),
                       deps=[ldw, prev_bw[par], last_bw], tag=f"red{hc}")
            al = p.op("vector", lambda e: e.tensor_scalar(out=alpha[:, 1:2], in0=alpha[:, 0:1],
                                                          scalar1=1.0 / D_, scalar2=None, op0=ALU.mult),
                      deps=[red], tag=f"al{hc}")
            sg = p.op("scalar", lambda e, par=par: e.activation(out=sgn[:], in_=wstg[par][:], func=AF.Sign),
                      deps=[ldw, al] + ([prev_bw[(hc - 1) % 2]] if hc >= 1 else []), tag=f"sg{hc}")
            prev_w[par] = sg
            bw = p.op("vector", lambda e, par=par: e.tensor_scalar(out=bwrow[par][:], in0=sgn[:],
                                                                   scalar1=alpha[:, 1:2], scalar2=None,
                                                                   op0=ALU.mult),
                      deps=[sg, al, prev_bw[par]], tag=f"bw{hc}")
            last_bw = bw
            evs = []
            for dc in range(8):
                sl = dc % 2
                tr = p.op("tensor",
                          lambda e, par=par, dc=dc, sl=sl: e.transpose(out=ps_tr[sl][:],
                                                                       in_=bwrow[par][:, dc * 128:(dc + 1) * 128],
                                                                       identity=ident[:]),
                          deps=[bw, ldid, prev_tr[sl]], tag=f"tr{hc}_{dc}")
                ev = p.op("vector",
                          lambda e, par=par, dc=dc, sl=sl: e.tensor_copy(
                              out=bwcol[par][:, dc * 128:(dc + 1) * 128], in_=ps_tr[sl][:]),
                          deps=[tr, prev_bwcol[par] if dc == 0 else None], tag=f"trev{hc}_{dc}")
                prev_tr[sl] = ev
                evs.append(ev)
            prev_bw[par] = evs[-1]
            for tb in range(4):
                slot = (hc * 4 + tb) % 2
                mms = []
                for kc in range(8):
                    mms.append(p.op("tensor",
                                    lambda e, kc=kc, par=par, tb=tb, slot=slot:
                                    e.matmul(out=ps[slot][:], lhsT=bwcol[par][:, kc * 128:(kc + 1) * 128],
                                             rhs=xnf[kc][:, tb * 512:(tb + 1) * 512],
                                             start=(kc == 0), stop=(kc == 7)),
                                    deps=(ldxn + [evs[-1], prev_ps[slot]]) if kc == 0 else [],
                                    tag=f"fm{hc}_{tb}_{kc}"))
                ev2 = p.op("scalar",
                           lambda e, hc=hc, slot=slot: e.activation(out=hres[slot][:], in_=ps[slot][:],
                                                                    func=AF.Gelu, bias=b1[:, hc:hc + 1]),
                           deps=[mms[-1], ldb1, prev_hres[slot]], tag=f"fev{hc}_{tb}")
                prev_ps[slot] = ev2
                sto = p.dma("gpsimd",
                            lambda e, hc=hc, tb=tb, slot=slot:
                            e.dma_start(out=h1_d[hc * 128:(hc + 1) * 128, tb * 512:(tb + 1) * 512],
                                        in_=hres[slot][:]),
                            deps=[ev2], tag=f"sth{hc}_{tb}", slot=f"hres{slot}")
                prev_hres[slot] = sto
                stores.append(sto)
            prev_bwcol[par] = mms[-1]
        p.barrier_op("gpsimd", stores, tag="final")
    finally:
        if es_own:
            es.close()


def build_H2():
    nc = bass.Bass("TRN2", target_bir_lowering=False, debug=False)
    io = _io_H2(nc)
    p = Prog(nc)
    p.slot_prefix = "H2."
    emit_H2(nc, p, io)
    with ExitStack() as es:
        block = es.enter_context(nc.Block())
        p.emit(es, block)
    return nc


def emit_H3a(nc, p, io):
    w2_d = io["w2"]
    id_d = io["ident"]
    bw2T_d = io["bw2T"]
    NH = HID1 // 128
    es_own = "es" not in io
    es = io.get("es") or ExitStack()
    try:
        wstg = [es.enter_context(nc.sbuf_tensor("H3a." + f"wstg{i}", [128, HID1], F32)) for i in range(2)]
        sgn = es.enter_context(nc.sbuf_tensor("H3a." + "sgn", [128, HID1], BF16))
        bwrow = [es.enter_context(nc.sbuf_tensor("H3a." + f"bwrow{i}", [128, HID1], BF16)) for i in range(2)]
        alpha = es.enter_context(nc.sbuf_tensor("H3a." + "alpha", [128, 4], F32))
        ident = es.enter_context(nc.sbuf_tensor("H3a." + "s_ident", [128, 128], BF16))
        tcol = [es.enter_context(nc.sbuf_tensor("H3a." + f"tcol{i}", [128, 128], BF16)) for i in range(2)]
        ps_tr = [es.enter_context(nc.psum_tensor("H3a." + f"pstr{i}", [128, 128], BF16)) for i in range(1)]
        ldid = p.dma("sync", lambda e: e.dma_start(out=ident[:], in_=id_d), tag="ldid", slot="f_init")
        prev_w = [None, None]
        prev_bw = [None, None]
        prev_tr = None
        last_bw = None
        prev_tcol = [None, None]
        stores = []
        for dc2 in range(8):
            par = dc2 % 2
            ldw = p.dma("sync", lambda e, dc2=dc2, par=par: e.dma_start(out=wstg[par][:], in_=w2_d[dc2 * 128:(dc2 + 1) * 128, :]),
                        deps=[prev_w[par]], tag=f"ldw{dc2}", slot=f"wstg{par}")
            red = p.op("vector", lambda e, par=par: e.tensor_reduce(out=alpha[:, 0:1], in_=wstg[par][:],
                                                                    axis=AX.X, op=ALU.add,
                                                                    apply_absolute_value=True),
                       deps=[ldw, last_bw], tag=f"red{dc2}")
            al = p.op("vector", lambda e: e.tensor_scalar(out=alpha[:, 1:2], in0=alpha[:, 0:1],
                                                          scalar1=1.0 / HID1, scalar2=None, op0=ALU.mult),
                      deps=[red], tag=f"al{dc2}")
            sg = p.op("scalar", lambda e, par=par: e.activation(out=sgn[:], in_=wstg[par][:], func=AF.Sign),
                      deps=[ldw, al] + ([prev_bw[(dc2 - 1) % 2]] if dc2 >= 1 else []), tag=f"sg{dc2}")
            prev_w[par] = sg
            bw = p.op("vector", lambda e, par=par: e.tensor_scalar(out=bwrow[par][:], in0=sgn[:],
                                                                   scalar1=alpha[:, 1:2], scalar2=None,
                                                                   op0=ALU.mult),
                      deps=[sg, al, prev_bw[par]], tag=f"bw{dc2}")
            last_bw = bw
            evs = []
            for hc in range(NH):
                sl = hc % 2
                tr = p.op("tensor",
                          lambda e, par=par, hc=hc: e.transpose(out=ps_tr[0][:],
                                                                in_=bwrow[par][:, hc * 128:(hc + 1) * 128],
                                                                identity=ident[:]),
                          deps=[bw, ldid, prev_tr], tag=f"tr{dc2}_{hc}")
                ev = p.op("vector",
                          lambda e, sl=sl: e.tensor_copy(out=tcol[sl][:], in_=ps_tr[0][:]),
                          deps=[tr, prev_tcol[sl]], tag=f"trev{dc2}_{hc}")
                prev_tr = ev
                sto = p.dma("gpsimd",
                            lambda e, dc2=dc2, hc=hc, sl=sl:
                            e.dma_start(out=bw2T_d[hc * 128:(hc + 1) * 128, dc2 * 128:(dc2 + 1) * 128],
                                        in_=tcol[sl][:]),
                            deps=[ev], tag=f"st{dc2}_{hc}", slot=f"tcol{sl}")
                prev_tcol[sl] = sto
                stores.append(sto)
                evs.append(ev)
            prev_bw[par] = evs[-1]
    finally:
        if es_own:
            es.close()


def _io_H3(nc):
    io = {}
    io["h1"] = nc.dram_tensor("h1", [HID1, TC], BF16, kind="ExternalInput").ap()
    io["w2"] = nc.dram_tensor("w2", [D_, HID1], F32, kind="ExternalInput").ap()
    io["b2bc"] = nc.dram_tensor("b2bc", [128, D_], F32, kind="ExternalInput").ap()
    io["x1"] = nc.dram_tensor("x1", [TC, D_], F32, kind="ExternalInput").ap()
    io["s1"] = nc.dram_tensor("s1", [TC, D_], BF16, kind="ExternalInput").ap()
    io["ident"] = nc.dram_tensor("ident", [128, 128], BF16, kind="ExternalInput").ap()
    io["xout"] = nc.dram_tensor("xout", [TC, D_], F32, kind="ExternalOutput").ap()
    return io


def emit_H3(nc, p, io):
    bw2T_d = io.get("bw2T")
    h1_d = io["h1"]
    w2_d = io["w2"]
    b2_d = io["b2bc"]
    x1_d = io["x1"]
    s1_d = io["s1"]
    id_d = io["ident"]
    out_d = io["xout"]
    NH = HID1 // 128
    NTC = TC // 128
    es_own = "es" not in io
    es = io.get("es") or ExitStack()
    try:
        wstg = [es.enter_context(nc.sbuf_tensor("H3." + f"wstg{i}", [128, HID1], F32)) for i in range(2)]
        sgn = es.enter_context(nc.sbuf_tensor("H3." + "sgn", [128, HID1], BF16))
        bwrow = [es.enter_context(nc.sbuf_tensor("H3." + f"bwrow{i}", [128, HID1], BF16)) for i in range(2)]
        alpha = es.enter_context(nc.sbuf_tensor("H3." + "alpha", [128, 4], F32))
        bw2T = [es.enter_context(nc.sbuf_tensor("H3." + f"bw2T{i}", [128, D_], BF16)) for i in range(NH)]
        b2bc = es.enter_context(nc.sbuf_tensor("H3." + "sb_b2", [128, D_], F32))
        ident = es.enter_context(nc.sbuf_tensor("H3." + "s_ident", [128, 128], BF16))
        hbuf = [es.enter_context(nc.sbuf_tensor("H3." + f"hbuf{i}", [128, NH * 128], BF16)) for i in range(2)]
        x1t = [es.enter_context(nc.sbuf_tensor("H3." + f"x1t{i}", [128, D_], F32)) for i in range(2)]
        s1t = [es.enter_context(nc.sbuf_tensor("H3." + f"s1t{i}", [128, D_], BF16)) for i in range(2)]
        tadd = es.enter_context(nc.sbuf_tensor("H3." + "tadd", [128, 512], F32))
        g1t = es.enter_context(nc.sbuf_tensor("H3." + "g1t", [128, 512], BF16))
        sgt = es.enter_context(nc.sbuf_tensor("H3." + "sgt", [128, 512], BF16))
        xot = [es.enter_context(nc.sbuf_tensor("H3." + f"xot{i}", [128, D_], F32)) for i in range(2)]
        ps_tr = [es.enter_context(nc.psum_tensor("H3." + f"pstr{i}", [128, 128], BF16)) for i in range(2)]
        ps = [es.enter_context(nc.psum_tensor("H3." + f"ps{i}", [128, 512], F32)) for i in range(2)]

        ldid = p.dma("sync", lambda e: e.dma_start(out=ident[:], in_=id_d), tag="ldid", slot="f_init")
        ldb2 = p.dma("sync", lambda e: e.dma_start(out=b2bc[:], in_=b2_d), tag="ldb2", slot="f_init")

        if bw2T_d is not None:
            lbs = [p.dma("sync", lambda e, kc=kc: e.dma_start(out=bw2T[kc][:], in_=bw2T_d[kc * 128:(kc + 1) * 128, :]),
                         tag=f"ldbw{kc}", slot="f_init") for kc in range(NH)]
            allbw = p.barrier_op("tensor", lbs, tag="bwdone")
        prev_w = [None, None]
        prev_bw = [None, None]
        prev_tr = [None, None]
        last_evs = {}
        for dc2 in (() if bw2T_d is not None else range(8)):
            par = dc2 % 2
            ldw = p.dma("sync", lambda e, dc2=dc2, par=par: e.dma_start(out=wstg[par][:], in_=w2_d[dc2 * 128:(dc2 + 1) * 128, :]),
                        deps=[prev_w[par]], tag=f"ldw{dc2}", slot=f"wstg{par}")
            red = p.op("vector", lambda e, par=par: e.tensor_reduce(out=alpha[:, 0:1], in_=wstg[par][:],
                                                                    axis=AX.X, op=ALU.add,
                                                                    apply_absolute_value=True),
                       deps=[ldw, last_evs.get(dc2 - 1)], tag=f"red{dc2}")
            al = p.op("vector", lambda e: e.tensor_scalar(out=alpha[:, 1:2], in0=alpha[:, 0:1],
                                                          scalar1=1.0 / HID1, scalar2=None, op0=ALU.mult),
                      deps=[red], tag=f"al{dc2}")
            sg = p.op("scalar", lambda e, par=par: e.activation(out=sgn[:], in_=wstg[par][:], func=AF.Sign),
                      deps=[ldw, al] + ([prev_bw[(dc2 - 1) % 2]] if dc2 >= 1 else []), tag=f"sg{dc2}")
            prev_w[par] = sg
            bw = p.op("vector", lambda e, par=par: e.tensor_scalar(out=bwrow[par][:], in0=sgn[:],
                                                                   scalar1=alpha[:, 1:2], scalar2=None,
                                                                   op0=ALU.mult),
                      deps=[sg, al, prev_bw[par]], tag=f"bw{dc2}")
            evs = []
            for hc in range(NH):
                sl = hc % 2
                tr = p.op("tensor",
                          lambda e, par=par, hc=hc, sl=sl: e.transpose(out=ps_tr[sl][:],
                                                                       in_=bwrow[par][:, hc * 128:(hc + 1) * 128],
                                                                       identity=ident[:]),
                          deps=[bw, ldid, prev_tr[sl]], tag=f"tr{dc2}_{hc}")
                ev = p.op("vector",
                          lambda e, dc2=dc2, hc=hc, sl=sl: e.tensor_copy(out=bw2T[hc][:, dc2 * 128:(dc2 + 1) * 128],
                                                                         in_=ps_tr[sl][:]),
                          deps=[tr], tag=f"trev{dc2}_{hc}")
                prev_tr[sl] = ev
                evs.append(ev)
            prev_bw[par] = evs[-1]
            last_evs[dc2] = evs[-1]
        if bw2T_d is None:
            allbw = p.barrier_op("tensor", [last_evs[d] for d in range(8)], tag="bwdone")

        stores = []
        prev_h = [None, None]
        prev_ps = [None, None]
        prev_out = [None, None]
        ev2 = None
        for tc in range(NTC):
            hpar = tc % 2
            lds = []
            for kc in range(NH):
                lds.append(p.dma("sync",
                                 lambda e, kc=kc, tc=tc, hpar=hpar:
                                 e.dma_start(out=hbuf[hpar][:, kc * 128:(kc + 1) * 128],
                                             in_=h1_d[kc * 128:(kc + 1) * 128, tc * 128:(tc + 1) * 128]),
                                 deps=[prev_h[hpar]] if kc == 0 else [], tag=f"ldh{tc}_{kc}",
                                 slot=f"hbuf{hpar}"))
            ldx1 = p.dma("sync", lambda e, tc=tc, hpar=hpar: e.dma_start(out=x1t[hpar][:], in_=x1_d[tc * 128:(tc + 1) * 128, :]),
                         deps=[prev_out[hpar]], tag=f"ldx1{tc}", slot=f"x1t{hpar}")
            lds1 = p.dma("sync", lambda e, tc=tc, hpar=hpar: e.dma_start(out=s1t[hpar][:], in_=s1_d[tc * 128:(tc + 1) * 128, :]),
                         deps=[prev_out[hpar]], tag=f"lds1{tc}", slot=f"s1t{hpar}")
            for ob in range(2):
                slot = ob
                mms = []
                for kc in range(NH):
                    mms.append(p.op("tensor",
                                    lambda e, kc=kc, ob=ob, hpar=hpar, slot=slot:
                                    e.matmul(out=ps[slot][:], lhsT=hbuf[hpar][:, kc * 128:(kc + 1) * 128],
                                             rhs=bw2T[kc][:, ob * 512:ob * 512 + 512],
                                             start=(kc == 0), stop=(kc == NH - 1)),
                                    deps=(lds + [allbw, prev_ps[slot]]) if kc == 0 else [],
                                    tag=f"fm{tc}_{ob}_{kc}"))
                ta = p.op("vector",
                          lambda e, ob=ob, slot=slot: e.tensor_add(out=tadd[:], in0=ps[slot][:],
                                                                   in1=b2bc[:, ob * 512:ob * 512 + 512]),
                          deps=[mms[-1], ldb2] + ([ev2] if ev2 is not None else []), tag=f"ta{tc}_{ob}")
                prev_ps[slot] = ta
                g1 = p.op("scalar", lambda e: e.activation(out=g1t[:], in_=tadd[:], func=AF.Silu),
                          deps=[ta], tag=f"g1_{tc}_{ob}")
                sgo = p.op("vector",
                           lambda e, ob=ob, hpar=hpar: e.tensor_mul(out=sgt[:], in0=s1t[hpar][:, ob * 512:ob * 512 + 512],
                                                                    in1=g1t[:]),
                           deps=[g1, lds1], tag=f"sg{tc}_{ob}")
                ev2 = p.op("vector",
                           lambda e, ob=ob, hpar=hpar: e.tensor_add(out=xot[hpar][:, ob * 512:ob * 512 + 512],
                                                                    in0=x1t[hpar][:, ob * 512:ob * 512 + 512],
                                                                    in1=sgt[:]),
                           deps=[sgo, ldx1, prev_out[hpar]], tag=f"x2_{tc}_{ob}")
                mlast = mms[-1]
            sto = p.dma("gpsimd",
                        lambda e, tc=tc, hpar=hpar: e.dma_start(out=out_d[tc * 128:(tc + 1) * 128, :], in_=xot[hpar][:]),
                        deps=[ev2], tag=f"sto{tc}", slot=f"xot{hpar}")
            prev_out[hpar] = sto
            prev_h[hpar] = mlast
            stores.append(sto)
        p.barrier_op("gpsimd", stores, tag="final")
    finally:
        if es_own:
            es.close()


def build_H3():
    nc = bass.Bass("TRN2", target_bir_lowering=False, debug=False)
    io = _io_H3(nc)
    p = Prog(nc)
    p.set_stage("H3.")
    emit_H3(nc, p, io)
    with ExitStack() as es:
        block = es.enter_context(nc.Block())
        p.emit(es, block)
    return nc


G4 = [[0, 1, 2, 3], [4, 5, 6, 7]]


def build_FUSED():
    nc = bass.Bass("TRN2", target_bir_lowering=False, debug=False)
    EI, EO = "ExternalInput", "ExternalOutput"
    d = {}

    def ext(name, shape, dt):
        d[name] = nc.dram_tensor(name, shape, dt, kind=EI).ap()

    def intr(name, shape, dt):
        d[name] = nc.dram_tensor(name, shape, dt).ap()

    ext("x", [TC, D_], F32)
    ext("wq", [D_, D_], F32); ext("wk", [D_, D_], F32)
    ext("wv", [D_, VD], F32); ext("wg", [D_, VD], F32)
    for nm in ("cosq", "sinq", "cosk", "sink"):
        ext(nm, [KD, TC], BF16)
    ext("rt", [128, 128], BF16); ext("ident", [128, 128], BF16)
    ext("wgT", [D_, HID0], I32); ext("wuT", [D_, HID0], I32); ext("wdT", [HID0, D_], I32)
    ext("gk", [128, H_ * CH], BF16); ext("gq", [128, H_ * CH], BF16)
    ext("dmT", [128, H_ * CH], BF16); ext("rden", [128, NCHUNK * H_], F32)
    ext("ones_col", [128, 1], BF16); ext("wo", [VD, D_], F32)
    ext("costm", [128, (TC // 128) * D_], BF16); ext("sintm", [128, (TC // 128) * D_], BF16)
    ext("gktm", [128, H_], F32)
    ext("coefD", [128, 4 * H_], F32)
    ext("nw", [128, 8], F32)
    ext("inwT", [D_, ESL], F32); ext("cw", [128, 16], F32); ext("cb", [128, 4], F32)
    ext("bcdw", [ESL, 96], F32)
    ext("wd2T", [DTR, ESL], F32); ext("wd2b", [128, 4], F32)
    ext("alog", [128, 4 * N_], F32); ext("wdsl", [128, 4], F32)
    ext("skT", [D_, E_], F32); ext("owT", [E_, D_], F32)
    ext("w1", [HID1, D_], F32); ext("b1", [128, HID1 // 128], F32)
    ext("w2", [D_, HID1], F32); ext("b2bc", [128, D_], F32)
    ext("yoff", [1, 1], I32)
    d["xout"] = nc.dram_tensor("xout", [TC, D_], F32, kind=EO).ap()

    intr("xn_fm", [D_, TC], BF16)
    intr("qr", [D_, TC], BF16); intr("kr", [D_, TC], BF16)
    intr("krtm", [TC, D_], BF16)
    intr("v", [TC, VD], BF16); intr("g", [TC, VD], BF16)
    intr("hid", [HID0, TC], BF16); intr("g0", [TC, D_], BF16)
    intr("t_out", [H_ * 128, HD + 1], F32)
    intr("t_cc", [4 * H_ * 128, HD + 1], F32)
    intr("s0", [TC, D_], BF16)
    intr("x1", [TC, D_], F32)
    intr("xn1_tm", [TC, D_], BF16); intr("xn1_fm", [D_, TC], BF16)
    intr("xn2_fm", [D_, TC], BF16)
    intr("xn2ag", [4 * D_, TC], BF16)
    intr("xc", [ESL, S_], BF16); intr("bcdp", [96, S_], F32)
    intr("bcd_sum", [96, S_], F32)
    bc_scr_t = nc.dram_tensor("bc_scr", [32, S_], BF16)
    intr("ypp", [ESL, S_], BF16)
    ypp_ag_t = nc.dram_tensor("ypp_ag", [E_, S_], BF16)
    intr("s1", [TC, D_], BF16); intr("h1", [HID1, TC], BF16)
    intr("bw2T", [HID1, D_], BF16)
    intr("skb16", [D_, E_], BF16); intr("owb16", [E_, D_], BF16)

    p = Prog(nc)

    def cc(kind, op, in_ap, out_ap, tag):
        return p.op("gpsimd",
                    lambda e, in_ap=in_ap, out_ap=out_ap, kind=kind, op=op:
                    e.collective_compute(kind, op, replica_groups=G4,
                                         ins=[in_ap.opt()], outs=[out_ap.opt()]),
                    deps=[], tag=tag)

    p.set_stage("A.")
    emit_A(nc, p, d)
    p.barrier_all("A")

    ses1 = ExitStack()

    def lane_b1():
        p.set_stage("B1.")
        emit_B1(nc, p, {"xn_fm": d["xn_fm"], "wgT": d["wgT"], "wuT": d["wuT"],
                        "hid": d["hid"], "es": ses1})

    def lane_c():
        p.set_stage("C.", "L1.")
        emit_C(nc, p, {"krtm": d["krtm"], "gktm": d["gktm"], "v": d["v"],
                       "gk": d["gk"], "ident": d["ident"],
                       "ones_col": d["ones_col"], "t_out": d["t_out"], "es": ses1})

    p.parallel([lane_b1, lane_c])
    p.barrier_all("P1a")
    ses1.close()
    cct = cc("AllGather", ALU.bypass, d["t_out"], d["t_cc"], "ccT")

    ses2 = ExitStack()

    def lane_b2():
        p.set_stage("B2.")
        emit_B2(nc, p, {"hid": d["hid"], "wdT": d["wdT"], "g0": d["g0"], "es": ses2})

    def lane_d():
        p.set_stage("D.", "L1.")
        emit_D(nc, p, {"qr": d["qr"], "kr": d["kr"], "krtm": d["krtm"],
                       "gktm": d["gktm"], "v": d["v"], "g": d["g"],
                       "t_cc": d["t_cc"], "coefD": d["coefD"], "ccdep": cct,
                       "gq": d["gq"], "gk": d["gk"], "dmT": d["dmT"], "rden": d["rden"],
                       "ident": d["ident"], "ones_col": d["ones_col"], "wo": d["wo"],
                       "s0": d["s0"], "es": ses2})

    p.parallel([lane_b2, lane_d])
    p.barrier_all("P1b")
    ses2.close()

    p.set_stage("E.")
    emit_E(nc, p, {"x": d["x"], "s0": d["s0"], "g0": d["g0"], "nw": d["nw"],
                   "ident": d["ident"], "x1": d["x1"], "xn1_tm": d["xn1_tm"],
                   "xn1_fm": d["xn1_fm"], "xn2_fm": d["xn2_fm"]})
    p.barrier_all("E")
    cc("AllGather", ALU.bypass, d["xn2_fm"], d["xn2ag"], "ccXN")
    p.barrier_all("ccXN")

    ses3 = ExitStack()

    def lane_f():
        p.set_stage("F.")
        emit_F(nc, p, {"xn2f": d["xn2ag"], "xn2_is_ag": 1,
                       "inwT": d["inwT"], "cw": d["cw"], "cb": d["cb"], "bcdw": d["bcdw"],
                       "xc": d["xc"], "bcdp": d["bcdp"], "es": ses3})

    def lane_h2():
        p.set_stage("H2.", "L1.")
        emit_H2(nc, p, {"xn1f": d["xn1_fm"], "w1": d["w1"], "b1": d["b1"],
                        "ident": d["ident"], "h1": d["h1"], "es": ses3})

    p.parallel([lane_f, lane_h2])
    p.barrier_all("P2")
    ses3.close()
    ccb = cc("AllReduce", ALU.add, d["bcdp"], d["bcd_sum"], "ccBCD")
    sesG = ExitStack()

    def lane_g():
        p.set_stage("G.")
        emit_G(nc, p, {"p0": d["bcd_sum"], "n_partials": 1, "ccdep": ccb,
                       "xc": d["xc"], "wd2T": d["wd2T"], "wd2b": d["wd2b"],
                       "alog": d["alog"], "wdsl": d["wdsl"], "ypp": d["ypp"],
                       "bc_scr": bc_scr_t, "es": sesG})

    def lane_hw():
        p.set_stage("H1w.", "L1.")
        emit_H1w(nc, p, {"skT": d["skT"], "owT": d["owT"],
                         "skb": d["skb16"], "owb": d["owb16"], "es": sesG})

    p.parallel([lane_g, lane_hw], chunk=[50, 2])
    p.barrier_all("G")
    sesG.close()
    sesH = ExitStack()
    ccy = cc("AllGather", ALU.bypass, d["ypp"], ypp_ag_t.ap(), "ccY")
    p.set_stage("H1.")
    ioH = {"ypp_ag_tensor": ypp_ag_t, "yoff": d["yoff"],
           "xn2f": d["xn2_fm"], "xn1tm": d["xn1_tm"], "cc": ccy,
           "skbd": d["skb16"], "owbd": d["owb16"], "s1": d["s1"], "es": sesH}
    stH = emit_H1a(nc, p, ioH)
    emit_H1b(nc, p, ioH, stH)
    p.barrier_all("H1")
    sesH.close()
    p.set_stage("H3.")
    emit_H3(nc, p, {"h1": d["h1"], "w2": d["w2"], "b2bc": d["b2bc"],
                    "x1": d["x1"], "s1": d["s1"], "ident": d["ident"],
                    "xout": d["xout"]})
    with ExitStack() as es:
        block = es.enter_context(nc.Block())
        p.emit(es, block)
    return nc


def _inputs_FUSED(inputs, c):
    C = _consts()
    gam = C["gamma"]
    m = dict(_inputs_A(inputs, c))
    m["wgT"] = np.ascontiguousarray(inputs["L0_ffn_w_gate"].T)
    m["wuT"] = np.ascontiguousarray(inputs["L0_ffn_w_up"].T)
    m["wdT"] = np.ascontiguousarray(inputs["L0_ffn_w_down"].T)
    m["gk"] = np.ascontiguousarray(C["gk_t"].transpose(1, 0, 2)).reshape(128, H_ * CH)
    m["gq"] = np.ascontiguousarray(C["gq_t"].transpose(1, 0, 2)).reshape(128, H_ * CH)
    m["dmT"] = np.ascontiguousarray(C["DmT"].transpose(1, 0, 2)).reshape(128, H_ * CH)
    t0 = _tok0(c)
    rden = np.empty((128, NCHUNK * H_), np.float32)
    for ccn in range(NCHUNK):
        for h in range(H_):
            rden[:, ccn * H_ + h] = C["Rtab"][h, t0 + ccn * 128:t0 + (ccn + 1) * 128]
    m["rden"] = rden
    m["ones_col"] = np.ones((128, 1), dtype=BF)
    m["wo"] = inputs["L0_attn_wo"]
    myrank = c % 4
    coef = np.zeros((128, 4 * H_), np.float32)
    for h in range(H_):
        for r in range(4):
            if r < myrank:
                coef[:, h * 4 + r] = gam[h] ** (TC * (myrank - r - 1))
    m["coefD"] = coef
    m["nw"] = np.ascontiguousarray(inputs["L1_attn_norm_w"].reshape(8, 128).T).astype(np.float32)
    e0 = (c % 4) * ESL
    inwT = np.ascontiguousarray(inputs["L1_attn_in_w"].T)
    wbT = np.ascontiguousarray(inputs["L1_attn_wb"].T)
    wcT = np.ascontiguousarray(inputs["L1_attn_wc"].T)
    wd1T = np.ascontiguousarray(inputs["L1_attn_wd1"].T)
    wd2T = np.ascontiguousarray(inputs["L1_attn_wd2"].T)
    m["inwT"] = np.ascontiguousarray(inwT[:, e0:e0 + ESL])
    m["cw"] = np.ascontiguousarray(
        inputs["L1_attn_conv_w"][e0:e0 + ESL].reshape(4, 128, 4).transpose(1, 0, 2)).reshape(128, 16).astype(np.float32)
    m["cb"] = np.ascontiguousarray(inputs["L1_attn_conv_b"][e0:e0 + ESL].reshape(4, 128).T).astype(np.float32)
    m["bcdw"] = np.ascontiguousarray(np.concatenate(
        [wd1T[e0:e0 + ESL], wbT[e0:e0 + ESL], wcT[e0:e0 + ESL]], axis=1)).astype(np.float32)
    m["wd2T"] = np.ascontiguousarray(wd2T[:, e0:e0 + ESL])
    m["wd2b"] = np.ascontiguousarray(inputs["L1_attn_wd2_b"][e0:e0 + ESL].reshape(4, 128).T).astype(np.float32)
    m["alog"] = np.ascontiguousarray(
        inputs["L1_attn_A_log"][e0:e0 + ESL].reshape(4, 128, N_).transpose(1, 0, 2)).reshape(128, 4 * N_).astype(np.float32)
    m["wdsl"] = np.ascontiguousarray(inputs["L1_attn_W_D"][e0:e0 + ESL].reshape(4, 128).T).astype(np.float32)
    m["skT"] = np.ascontiguousarray(inputs["L1_attn_skip_w"].T).astype(np.float32)
    m["owT"] = np.ascontiguousarray(inputs["L1_attn_out_w"].T).astype(np.float32)
    m["w1"] = inputs["L1_ffn_w1"]
    m["b1"] = np.ascontiguousarray(inputs["L1_ffn_b1"].reshape(HID1 // 128, 128).T).astype(np.float32)
    m["w2"] = inputs["L1_ffn_w2"]
    m["b2bc"] = np.ascontiguousarray(np.broadcast_to(inputs["L1_ffn_b2"][None, :], (128, D_))).astype(np.float32)
    m["yoff"] = np.array([[(c % 4) * TC]], np.int32)
    m["gktm"] = C["gktm"]
    return m


FUSED = True


def kernel_fused(inputs):
    nc = _get("FUSED", build_FUSED)
    maps = [_inputs_FUSED(inputs, c) for c in range(NCORES)]
    res = run_bass_kernel_spmd(nc, maps, core_ids=list(range(NCORES))).results
    out = np.empty((B_, S_, D_), np.float32)
    for c in range(NCORES):
        b, t0 = c // 4, _tok0(c)
        out[b, t0:t0 + TC] = np.asarray(res[c]["xout"])
    return out


_BUILD_CACHE = {}


def _get(name, fn):
    if name not in _BUILD_CACHE:
        _BUILD_CACHE[name] = fn()
    return _BUILD_CACHE[name]


def _run(name, buildfn, maps):
    nc = _get(name, buildfn)
    return run_bass_kernel_spmd(nc, maps, core_ids=list(range(NCORES))).results


def run_layer0(inputs):
    C = _consts()
    resA = _run("A", build_A, [_inputs_A(inputs, c) for c in range(NCORES)])

    wgT = np.ascontiguousarray(inputs["L0_ffn_w_gate"].T)
    wuT = np.ascontiguousarray(inputs["L0_ffn_w_up"].T)
    wdT = np.ascontiguousarray(inputs["L0_ffn_w_down"].T)
    resB1 = _run("B1", build_B1, [
        {"xn_fm": resA[c]["xn_fm"], "wgT": wgT, "wuT": wuT} for c in range(NCORES)])
    resB2 = _run("B2", build_B2, [
        {"hid": resB1[c]["hid"], "wdT": wdT} for c in range(NCORES)])

    gk2 = np.ascontiguousarray(C["gk_t"].transpose(1, 0, 2)).reshape(128, H_ * CH)
    gq2 = np.ascontiguousarray(C["gq_t"].transpose(1, 0, 2)).reshape(128, H_ * CH)
    dm2 = np.ascontiguousarray(
        np.broadcast_to(C["DmT"][:, None, :, :], (H_, 128, CH, CH))[:, :, :, :]
    ) if False else None
    dm2 = np.ascontiguousarray(C["DmT"].transpose(1, 0, 2)).reshape(128, H_ * CH)
    ones_col = np.ones((128, 1), dtype=BF)
    resC = _run("C", build_C, [
        {"krtm": resA[c]["krtm"], "gktm": C["gktm"], "kr": resA[c]["kr"],
         "v": resA[c]["v"], "gk": gk2,
         "ident": C["ident"], "ones_col": ones_col} for c in range(NCORES)])

    mapsD = []
    for c in range(NCORES):
        j = c % 4
        tp = np.zeros((3, H_, 128, HD + 1), np.float32)
        for m in range(3):
            pc = c - 1 - m
            if pc >= (c // 4) * 4:
                tp[m] = np.asarray(resC[pc]["t_out"]).reshape(H_, 128, HD + 1)
        tp = np.ascontiguousarray(tp.transpose(2, 1, 0, 3)).reshape(128, H_ * 3 * (HD + 1))
        t0 = _tok0(c)
        rden = np.empty((128, NCHUNK * H_), np.float32)
        for cc in range(NCHUNK):
            for h in range(H_):
                rden[:, cc * H_ + h] = C["Rtab"][h, t0 + cc * 128:t0 + (cc + 1) * 128]
        mapsD.append({
            "krtm": resA[c]["krtm"], "gktm": C["gktm"],
            "qr": resA[c]["qr"], "kr": resA[c]["kr"], "v": resA[c]["v"],
            "g": resA[c]["g"], "t_prev": tp,
            "gq": gq2, "gk": gk2, "dmT": dm2, "rden": rden,
            "ident": C["ident"], "ones_col": ones_col,
            "wo": inputs["L0_attn_wo"],
        })
    resD = _run("D", build_D, mapsD)

    nw = np.ascontiguousarray(inputs["L1_attn_norm_w"].reshape(8, 128).T)
    mapsE = []
    for c in range(NCORES):
        b, t0 = c // 4, _tok0(c)
        x = np.ascontiguousarray(inputs["L0_attn_x"][b, t0:t0 + TC]).astype(np.float32)
        mapsE.append({"x": x, "s0": resD[c]["s0"], "g0": resB2[c]["g0"],
                      "nw": nw, "ident": C["ident"]})
    resE = _run("E", build_E, mapsE)
    return resA, resB1, resB2, resC, resD, resE


def run_layer1(inputs, resE):
    C = _consts()
    xn2_full = [np.concatenate([np.asarray(resE[4 * b + j]["xn2_fm"]) for j in range(4)], axis=1)
                for b in range(2)]
    inwT = np.ascontiguousarray(inputs["L1_attn_in_w"].T)
    skT = np.ascontiguousarray(inputs["L1_attn_skip_w"].T)
    owT = np.ascontiguousarray(inputs["L1_attn_out_w"].T)
    wbT = np.ascontiguousarray(inputs["L1_attn_wb"].T)
    wcT = np.ascontiguousarray(inputs["L1_attn_wc"].T)
    wd1T = np.ascontiguousarray(inputs["L1_attn_wd1"].T)
    wd2T = np.ascontiguousarray(inputs["L1_attn_wd2"].T)

    mapsF = []
    for c in range(NCORES):
        b = c // 4
        e0 = (c % 4) * ESL
        cw = np.ascontiguousarray(
            inputs["L1_attn_conv_w"][e0:e0 + ESL].reshape(4, 128, 4).transpose(1, 0, 2)).reshape(128, 16)
        cb = np.ascontiguousarray(inputs["L1_attn_conv_b"][e0:e0 + ESL].reshape(4, 128).T)
        bcdw = np.ascontiguousarray(np.concatenate(
            [wd1T[e0:e0 + ESL], wbT[e0:e0 + ESL], wcT[e0:e0 + ESL]], axis=1))
        mapsF.append({"xn2f": xn2_full[b],
                      "inwT": np.ascontiguousarray(inwT[:, e0:e0 + ESL]),
                      "cw": cw.astype(np.float32), "cb": cb.astype(np.float32),
                      "bcdw": bcdw.astype(np.float32)})
    resF = _run("F", build_F, mapsF)

    mapsG = []
    for c in range(NCORES):
        b = c // 4
        e0 = (c % 4) * ESL
        m = {"xc": resF[c]["xc"],
             "wd2T": np.ascontiguousarray(wd2T[:, e0:e0 + ESL]),
             "wd2b": np.ascontiguousarray(inputs["L1_attn_wd2_b"][e0:e0 + ESL].reshape(4, 128).T).astype(np.float32),
             "alog": np.ascontiguousarray(
                 inputs["L1_attn_A_log"][e0:e0 + ESL].reshape(4, 128, N_).transpose(1, 0, 2)).reshape(128, 4 * N_).astype(np.float32),
             "wdsl": np.ascontiguousarray(inputs["L1_attn_W_D"][e0:e0 + ESL].reshape(4, 128).T).astype(np.float32)}
        for j in range(4):
            m[f"p{j}"] = resF[4 * b + j]["bcdp"]
        mapsG.append(m)
    resG = _run("G", build_G, mapsG)

    ypp_full = [np.concatenate([np.asarray(resG[4 * b + j]["ypp"]) for j in range(4)], axis=0)
                for b in range(2)]
    mapsH1 = []
    for c in range(NCORES):
        b, t0 = c // 4, _tok0(c)
        mapsH1.append({"yppt": np.ascontiguousarray(ypp_full[b][:, t0:t0 + TC]),
                       "xn2f": resE[c]["xn2_fm"], "xn1tm": resE[c]["xn1_tm"],
                       "skT": skT.astype(np.float32), "owT": owT.astype(np.float32)})
    resH1 = _run("H1", build_H1, mapsH1)

    b1r = np.ascontiguousarray(inputs["L1_ffn_b1"].reshape(HID1 // 128, 128).T).astype(np.float32)
    mapsH2 = [{"xn1f": resE[c]["xn1_fm"], "w1": inputs["L1_ffn_w1"],
               "b1": b1r, "ident": C["ident"]} for c in range(NCORES)]
    resH2 = _run("H2", build_H2, mapsH2)

    b2bc = np.ascontiguousarray(np.broadcast_to(inputs["L1_ffn_b2"][None, :], (128, D_))).astype(np.float32)
    mapsH3 = [{"h1": resH2[c]["h1"], "w2": inputs["L1_ffn_w2"], "b2bc": b2bc,
               "x1": resE[c]["x1"], "s1": resH1[c]["s1"], "ident": C["ident"]}
              for c in range(NCORES)]
    resH3 = _run("H3", build_H3, mapsH3)
    return resF, resG, resH1, resH2, resH3


def kernel(**inputs):
    inputs = {k: np.asarray(v) for k, v in inputs.items()}
    if FUSED:
        return kernel_fused(inputs)
    resA, resB1, resB2, resC, resD, resE = run_layer0(inputs)
    resF, resG, resH1, resH2, resH3 = run_layer1(inputs, resE)
    out = np.empty((B_, S_, D_), np.float32)
    for c in range(NCORES):
        b, t0 = c // 4, _tok0(c)
        out[b, t0:t0 + TC] = np.asarray(resH3[c]["xout"])
    return out
